# version 21
# baseline (speedup 1.0000x reference)
import contextlib
import math

import numpy as np
import ml_dtypes

import concourse.bass as bass
import concourse.mybir as mybir
from concourse.bass import IndirectOffsetOnAxis
from concourse.bass_utils import run_bass_kernel_spmd

F32 = mybir.dt.float32
BF16 = mybir.dt.bfloat16
I32 = mybir.dt.int32
AF = mybir.ActivationFunctionType
ALU = mybir.AluOpType
AX = mybir.AxisListType

B, S, D, H, DH = 2, 4096, 2048, 16, 128
FF = 8192
T = B * S
TC = T // 8
GROUPS4 = [[0, 1, 2, 3], [4, 5, 6, 7]]
GROUP8 = [list(range(8))]
RS128 = float(np.float32(1.0 / math.sqrt(128.0)))
CLIPRAW = float(np.float32(1024.0 * math.sqrt(128.0)))
NEGBIG = -1.0e30
QBW = 512
NQB = S // QBW
NKC = S // 128

SLOT_FD = [64, 256, 512, 512]
SLOT_WIN = [832, None, None, None]


def _alibi_slopes(n):
    def pow2(m):
        start = 2 ** (-(2 ** (-(math.log2(m) - 3))))
        return [start * start ** i for i in range(m)]
    if math.log2(n).is_integer():
        return pow2(n)
    c = 2 ** math.floor(math.log2(n))
    return pow2(c) + _alibi_slopes(2 * c)[0::2][: n - c]


SLOPES = np.array(_alibi_slopes(H), dtype=np.float32)


def kc_range(s, qb):
    hi = 4 * qb + 4
    w = SLOT_WIN[s]
    lo = 0 if w is None else max(0, (QBW * qb - w) // 128)
    return list(range(lo, hi))


def slot_cols(s):
    fd = SLOT_FD[s]
    return [128 * kc - (QBW * qb + qs * fd)
            for qb in range(NQB) for kc in kc_range(s, qb)
            for qs in range(QBW // fd)]


def build_bias_table(head):
    sig = SLOPES[head]
    offs = np.array(slot_cols(head // 4), dtype=np.float32)
    jp = np.arange(128, dtype=np.float32)[:, None]
    return (np.float32(sig) * (jp + offs[None, :])).astype(np.float32)


def build_tiles():
    tiles = []
    for u in range(4):
        col = 0
        for qb in range(NQB):
            kcs = kc_range(u, qb)
            for kc in kcs:
                tiles.append(dict(u=u, qb=qb, kc=kc,
                                  first=(kc == kcs[0]), last=(kc == kcs[-1]),
                                  diag=(kc >= 4 * qb), col0=col))
                col += QBW // SLOT_FD[u]
    return tiles


TILES = build_tiles()
NT = len(TILES)
NDIAG = []
_nd = 0
for _t in TILES:
    if _t["diag"]:
        _nd += 1
    NDIAG.append(_nd)
QBIDX = [t["u"] * NQB + t["qb"] for t in TILES]
UNIT_T = [[i for i, t in enumerate(TILES) if t["u"] == u] for u in range(4)]
DIAGS = [i for i, t in enumerate(TILES) if t["diag"]]


_DEBUG = False


def build_program():
    nc = bass.Bass("TRN2", target_bir_lowering=False)

    x_own = nc.dram_tensor("x_own", [TC, D], F32, kind="ExternalInput").ap()
    wq_u = nc.dram_tensor("wq_u", [4 * 128, D], F32, kind="ExternalInput").ap()
    wk_u = nc.dram_tensor("wk_u", [4 * 128, D], F32, kind="ExternalInput").ap()
    wv_u = nc.dram_tensor("wv_u", [4 * 128, D], F32, kind="ExternalInput").ap()
    wo_own = nc.dram_tensor("wo_own", [256, D], F32, kind="ExternalInput").ap()
    w1_own = nc.dram_tensor("w1_own", [1024, D], F32, kind="ExternalInput").ap()
    w2_own = nc.dram_tensor("w2_own", [256, FF], F32, kind="ExternalInput").ap()
    b1_in = nc.dram_tensor("b1_in", [128, FF // 128], F32, kind="ExternalInput").ap()
    b2_in = nc.dram_tensor("b2_in", [128, D // 128], F32, kind="ExternalInput").ap()
    id_bf = nc.dram_tensor("id_bf", [128, 128], BF16, kind="ExternalInput").ap()
    id_f32 = nc.dram_tensor("id_f32", [128, 128], F32, kind="ExternalInput").ap()
    btabs_in = [nc.dram_tensor(f"btab{s}", [128, len(slot_cols(s))], F32,
                               kind="ExternalInput").ap() for s in range(4)]
    masks_in = nc.dram_tensor("masks", [128, 4 * 512], F32, kind="ExternalInput").ap()
    zidx_in = nc.dram_tensor("zidx", [128, 32], I32, kind="ExternalInput").ap()
    xnidx_in = nc.dram_tensor("xnidx", [128, 32], I32, kind="ExternalInput").ap()
    out_own = nc.dram_tensor("out_own", [TC, D], F32, kind="ExternalOutput").ap()

    cc_sc_in = nc.dram_tensor("cc_sc_in", [1, 2], F32, kind="Internal").ap()
    cc_sc_out = nc.dram_tensor("cc_sc_out", [1, 2], F32, kind="Internal").ap()
    cc_w1_in = nc.dram_tensor("cc_w1_in", [D, 1024], BF16, kind="Internal").ap()
    g_w1qT = nc.dram_tensor("g_w1qT", [8 * D, 1024], BF16, kind="Internal").ap()
    cc_w2_in = nc.dram_tensor("cc_w2_in", [FF, 256], BF16, kind="Internal").ap()
    g_w2qT = nc.dram_tensor("g_w2qT", [8 * FF, 256], BF16, kind="Internal").ap()
    cc_wo_in = nc.dram_tensor("cc_wo_in", [D, 256], BF16, kind="Internal").ap()
    g_woT = nc.dram_tensor("g_woT", [8 * D, 256], BF16, kind="Internal").ap()
    cc_xnh = [nc.dram_tensor(f"cc_xnh{h}", [D, 512], BF16, kind="Internal").ap()
              for h in range(2)]
    g_xnh = [nc.dram_tensor(f"g_xnh{h}", [4 * D, 512], BF16, kind="Internal").ap()
             for h in range(2)]
    cc_zh = [nc.dram_tensor(f"cc_zh{h}", [256, S], BF16, kind="Internal").ap()
             for h in range(2)]
    g_zh = [nc.dram_tensor(f"g_zh{h}", [1024, S], BF16, kind="Internal").ap()
            for h in range(2)]
    l_wT = nc.dram_tensor("l_wT", [12 * D, 128], BF16, kind="Internal").ap()
    if _DEBUG:
        d_gxn = nc.dram_tensor("d_gxn", [4 * D, TC], BF16, kind="ExternalOutput").ap()
        d_gz = nc.dram_tensor("d_gz", [D, S], BF16, kind="ExternalOutput").ap()
        d_w1 = nc.dram_tensor("d_w1", [D, 1024], BF16, kind="ExternalOutput").ap()
        d_sc = nc.dram_tensor("d_sc", [1, 2], F32, kind="ExternalOutput").ap()

    with contextlib.ExitStack() as ctx:
        E = ctx.enter_context
        SEMS = [
            "ldc",
            "la0", "la1", "abs0", "vs0", "mm1", "part0", "gd0", "inv0", "bc0", "invb0",
            "lq0", "lq1", "q0", "qm", "qh", "tp0", "ev0", "sq",
            "lc0", "lc1", "tp0b", "ev0b", "stw0b",
            "lx0", "lx1", "lx2", "lx3", "lx4", "lx5", "lx6", "lx7", "stats1", "var1", "std1", "rstd1", "xnb1", "tp1", "ev1", "st1",
            "wld2", "ln0", "ln1", "mqk2", "mv2", "eqk2", "evv2",
            "s3", "c3", "g3", "e3", "av3", "z3", "ztp3", "zev3", "zst3",
            "l40", "l41", "xg4", "mm4", "gel4",
            "xl5", "l20", "l21", "lo0", "lo1", "zg5", "mm5", "sc5", "tp5", "add5",
            "os0", "os1",
            "cc", "vf", "gf", "scf",
        ]
        sh = {nm: E(nc.semaphore(nm)) for nm in SEMS}
        block = E(nc.Block())
        _vf = [0]
        _gf = [0]

        def vfence(eng, instr):
            instr.then_inc(sh["vf"], 1)
            _vf[0] += 1
            eng.wait_ge(sh["vf"], _vf[0])

        def gfence(eng, instr):
            instr.then_inc(sh["gf"], 1)
            _gf[0] += 1
            eng.wait_ge(sh["gf"], _gf[0])

        CC_XN0, CC_AR, CC_XN1, CC_W1, CC_W2, CC_Z01, CC_WO, CC_Z23 = \
            1, 2, 3, 4, 5, 6, 7, 8

        s_idbf = E(nc.sbuf_tensor("s_idbf", [128, 128], BF16))
        s_idf32 = E(nc.sbuf_tensor("s_idf32", [128, 128], F32))
        s_btab = [E(nc.sbuf_tensor(f"s_btab{s}", [128, len(slot_cols(s))], F32))
                  for s in range(4)]
        s_b1 = E(nc.sbuf_tensor("s_b1", [128, FF // 128], F32))
        s_b2 = E(nc.sbuf_tensor("s_b2", [128, D // 128], F32))
        s_mask = E(nc.sbuf_tensor("s_mask", [128, 4 * 512], F32))
        s_zidx = E(nc.sbuf_tensor("s_zidx", [128, 32], I32))
        s_xnidx = E(nc.sbuf_tensor("s_xnidx", [128, 32], I32))
        s_invb = E(nc.sbuf_tensor("s_invb", [128, 2], F32))
        s_onecol = E(nc.sbuf_tensor("s_onecol", [128, 1], F32))
        s_onerow = E(nc.sbuf_tensor("s_onerow", [1, 128], F32))

        @block.sync
        def _(sy):
            sy.dma_start(out=s_idbf[:], in_=id_bf).then_inc(sh["ldc"], 16)
            sy.dma_start(out=s_idf32[:], in_=id_f32).then_inc(sh["ldc"], 16)
            for s in range(4):
                sy.dma_start(out=s_btab[s][:], in_=btabs_in[s]).then_inc(sh["ldc"], 16)
            sy.dma_start(out=s_b1[:], in_=b1_in).then_inc(sh["ldc"], 16)
            sy.dma_start(out=s_b2[:], in_=b2_in).then_inc(sh["ldc"], 16)
            sy.dma_start(out=s_mask[:], in_=masks_in).then_inc(sh["ldc"], 16)
            sy.dma_start(out=s_zidx[:], in_=zidx_in).then_inc(sh["ldc"], 16)
            sy.dma_start(out=s_xnidx[:], in_=xnidx_in).then_inc(sh["ldc"], 16)
        NCONSTS = 16 * 11

        wprep = contextlib.ExitStack()
        E0 = wprep.enter_context
        p0_w = [E0(nc.sbuf_tensor(f"p0_w{i}", [128, D], F32)) for i in range(2)]
        p0_absco = E0(nc.sbuf_tensor("p0_absco", [128, 16], F32))
        p0_vsum = E0(nc.sbuf_tensor("p0_vsum", [128, 2], F32))
        p0_scp = E0(nc.sbuf_tensor("p0_scp", [1, 2], F32))
        p0_sct = E0(nc.sbuf_tensor("p0_sct", [1, 2], F32))
        p0_inv2 = E0(nc.sbuf_tensor("p0_inv2", [1, 2], F32))
        p0_qf = E0(nc.sbuf_tensor("p0_qf", [128, D // 2], F32))
        p0_qi = E0(nc.sbuf_tensor("p0_qi", [128, D // 2], I32))
        p0_qb = E0(nc.sbuf_tensor("p0_qb", [128, D], BF16))
        p0_wt = E0(nc.sbuf_tensor("p0_wt", [128, D], BF16))
        _wp_psg = [nc.psum_tensor(f"ps_tp{i}", [128, 1024], BF16) for i in range(2)]
        _wp_psg.append(nc.psum_tensor("ps_sc", [128, 512], F32))
        ps_tp = [_wp_psg[0].__enter__(), _wp_psg[1].__enter__()]
        ps_sc = _wp_psg[2].__enter__()

        def w_src(i):
            if i < 8:
                return w1_own[i * 128:(i + 1) * 128, :]
            j = i - 8
            r2, c2 = j // 4, j % 4
            return w2_own[r2 * 128:(r2 + 1) * 128, c2 * D:(c2 + 1) * D]

        def w_dst(i):
            if i < 8:
                return cc_w1_in[:, i * 128:(i + 1) * 128]
            j = i - 8
            r2, c2 = j // 4, j % 4
            return cc_w2_in[c2 * D:(c2 + 1) * D, r2 * 128:(r2 + 1) * 128]

        def wc_src(i):
            if i < 2:
                return wo_own[i * 128:(i + 1) * 128, :]
            w, u = (i - 2) // 4, (i - 2) % 4
            return [wq_u, wk_u, wv_u][w][u * 128:(u + 1) * 128, :]

        p1 = contextlib.ExitStack()
        E1 = p1.enter_context
        p0_wc = [E1(nc.sbuf_tensor(f"p0_wc{i}", [128, D], BF16)) for i in range(2)]

        @block.gpsimd
        def _(g):
            for i in range(14):
                if i >= 2:
                    g.wait_ge(sh["tp0b"], 4 * (i - 1))
                g.dma_start(out=p0_wc[i % 2][:], in_=wc_src(i)).then_inc(sh[f"lc{i % 2}"], 16)

        FRONT = []
        _ci = 0
        for _li in range(8):
            FRONT.append(("L", _li))
            for _ in range(2):
                if _ci < 14:
                    FRONT.append(("C", _ci))
                    _ci += 1
        while _ci < 14:
            FRONT.append(("C", _ci))
            _ci += 1

        p1_x = [E1(nc.sbuf_tensor(f"p1_x{i}", [128, D], F32)) for i in range(8)]
        p1_scr = E1(nc.sbuf_tensor("p1_scr", [128, D], F32))
        p1_xnb = E1(nc.sbuf_tensor("p1_xnb", [128, D], BF16))
        p1_sums = E1(nc.sbuf_tensor("p1_sums", [128, 2], F32))
        p1_mean = E1(nc.sbuf_tensor("p1_mean", [128, 1], F32))
        p1_var = E1(nc.sbuf_tensor("p1_var", [128, 1], F32))
        p1_std = E1(nc.sbuf_tensor("p1_std", [128, 1], F32))
        p1_rstd = E1(nc.sbuf_tensor("p1_rstd", [128, 1], F32))
        p1_nmr = E1(nc.sbuf_tensor("p1_nmr", [128, 1], F32))
        p1_tmp = E1(nc.sbuf_tensor("p1_tmp", [128, 1], F32))
        p1_m2 = E1(nc.sbuf_tensor("p1_m2", [128, 1], F32))
        s_xnT = E1(nc.sbuf_tensor("s_xnT", [128, 16 * TC], BF16))
        p1_tpg = [nc.psum_tensor(f"p1_tp{i}", [128, 1024], BF16) for i in range(2)]
        p1_tp = [g.__enter__() for g in p1_tpg]

        @block.sync
        def _(sy):
            for i in range(8):
                sy.dma_start(out=p1_x[i][:], in_=x_own[i * 128:(i + 1) * 128, :]).then_inc(sh[f"lx{i}"], 16)
            for i in range(14):
                sy.wait_ge(sh["ev0b"], 4 * (i + 1))
                if i >= 1:
                    sy.wait_ge(sh["stw0b"], 16 * i)
                if i < 2:
                    dst = cc_wo_in[:, i * 128:(i + 1) * 128]
                else:
                    dst = l_wT[(i - 2) * D:(i - 1) * D, :]
                sy.dma_start(out=dst.rearrange("(c p) f -> p c f", p=128),
                             in_=p0_wt[:].rearrange("p (c f) -> p c f", c=16)).then_inc(sh["stw0b"], 16)
            for h in range(2):
                sy.wait_ge(sh["ev1"], 16 * (h + 1))
                if h >= 1:
                    sy.wait_ge(sh["st1"], 16 * h)
                sy.dma_start(out=cc_xnh[h].rearrange("(c p) t -> p c t", p=128),
                             in_=s_xnT[:].rearrange("p (c t) -> p c t", c=16)[:, :, h * 512:(h + 1) * 512]).then_inc(sh["st1"], 16)

        @block.scalar
        def _(sc):
            for i in range(2):
                sc.dma_start(out=p0_w[i % 2][:], in_=w_src(i)).then_inc(sh[f"la{i % 2}"], 16)
            def emit_abs(t):
                sc.wait_ge(sh[f"la{t % 2}"], 16 * (t // 2 + 1))
                if t % 2 == 0:
                    sc.wait_ge(sh["stats1"], t // 2 + 1)
                if t >= 1:
                    sc.wait_ge(sh["abs0"], t)
                sc.activation(p1_scr[:], p0_w[t % 2][:], AF.Abs,
                              accum_out=p0_absco[:, t:t + 1]).then_inc(sh["abs0"], 1)
                if t + 2 < 16:
                    sc.wait_ge(sh["abs0"], t + 1)
                    sc.dma_start(out=p0_w[t % 2][:], in_=w_src(t + 2)).then_inc(sh[f"la{t % 2}"], 16)
            for i in range(8):
                sc.wait_ge(sh[f"lx{i}"], 16)
                if i >= 1:
                    sc.wait_ge(sh["var1"], i)
                    sc.wait_ge(sh["abs0"], 2 * i)
                sc.activation(p1_scr[:], p1_x[i][:], AF.Identity,
                              accum_out=p1_sums[:, 0:1]).then_inc(sh["scf"], 1)
                sc.wait_ge(sh["scf"], i + 1)
                sc.activation(p1_scr[:], p1_x[i][:], AF.Square,
                              accum_out=p1_sums[:, 1:2]).then_inc(sh["stats1"], 1)
                emit_abs(2 * i)
                sc.wait_ge(sh["var1"], i + 1)
                sc.activation(p1_std[:], p1_var[:], AF.Sqrt).then_inc(sh["std1"], 1)
                emit_abs(2 * i + 1)
                sc.wait_ge(sh["rstd1"], i + 1)
                if i >= 1:
                    sc.wait_ge(sh["tp1"], 4 * i)
                sc.activation(p1_xnb[:], p1_x[i][:], AF.Identity,
                              bias=p1_nmr[:], scale=p1_rstd[:]).then_inc(sh["xnb1"], 1)
            for i in range(16):
                for hh in range(2):
                    j = 2 * i + hh
                    if j == 0:
                        sc.wait_ge(sh["invb0"], 1)
                    sc.wait_ge(sh[f"lq{i % 2}"], 16 * (i // 2 + 1))
                    if j >= 1:
                        sc.wait_ge(sh["qh"], j)
                    m = 0 if i < 8 else 1
                    sl = slice(hh * (D // 2), (hh + 1) * (D // 2))
                    sc.activation(p0_qf[:], p0_w[i % 2][:, sl], AF.Identity,
                                  scale=s_invb[:, m:m + 1]).then_inc(sh["qm"], 1)

        @block.vector
        def _(ve):
            for kind, i in FRONT:
                if kind == "C":
                    for g4 in range(4):
                        G = 4 * i + g4
                        ve.wait_ge(sh["tp0b"], G + 1)
                        if i >= 1 and g4 == 0:
                            ve.wait_ge(sh["stw0b"], 16 * i)
                        ve.tensor_copy(p0_wt[:, g4 * 512:(g4 + 1) * 512],
                                       ps_tp[g4 % 2][:, 0:512]).then_inc(sh["ev0b"], 1)
                    continue
                ve.wait_ge(sh["stats1"], i + 1)
                ve.tensor_scalar(p1_mean[:], p1_sums[:, 0:1], 1.0 / D, None, ALU.mult)
                vfence(ve, ve.tensor_scalar(p1_tmp[:], p1_sums[:, 1:2], 1.0 / D, None, ALU.mult))
                vfence(ve, ve.tensor_tensor(p1_m2[:], p1_mean[:], p1_mean[:], ALU.mult))
                vfence(ve, ve.tensor_sub(p1_var[:], p1_tmp[:], p1_m2[:]))
                ve.tensor_scalar(p1_var[:], p1_var[:], 1e-5, None, ALU.add).then_inc(sh["var1"], 1)
                ve.wait_ge(sh["std1"], i + 1)
                vfence(ve, ve.reciprocal(p1_rstd[:], p1_std[:]))
                ve.scalar_tensor_tensor(p1_nmr[:], p1_mean[:], -1.0, p1_rstd[:],
                                        ALU.mult, ALU.mult).then_inc(sh["rstd1"], 1)
                for g4 in range(4):
                    G = 4 * i + g4
                    ve.wait_ge(sh["tp1"], G + 1)
                    for c4 in range(4):
                        c = g4 * 4 + c4
                        ins = ve.tensor_copy(
                            s_xnT[:, c * TC + i * 128: c * TC + (i + 1) * 128],
                            p1_tp[g4 % 2][:, c4 * 128:(c4 + 1) * 128])
                    ins.then_inc(sh["ev1"], 1)
            ve.wait_ge(sh["abs0"], 16)
            ve.memset(s_onecol[:], 1.0)
            ve.memset(s_onerow[:], 1.0)
            ve.reduce_sum(p0_vsum[:, 0:1], p0_absco[:, 0:8], axis=AX.X)
            ve.reduce_sum(p0_vsum[:, 1:2], p0_absco[:, 8:16], axis=AX.X).then_inc(sh["vs0"], 1)
            ve.wait_ge(sh["mm1"], 1)
            ve.tensor_copy(p0_scp[:], ps_sc[0:1, 0:2]).then_inc(sh["part0"], 1)
            ve.wait_ge(sh["gd0"], 32)
            vfence(ve, ve.tensor_scalar(p0_sct[:], p0_sct[:], 1.0 / float(FF * D), 1e-8,
                                        ALU.mult, ALU.max))
            ve.reciprocal(p0_inv2[:], p0_sct[:]).then_inc(sh["inv0"], 1)
            ve.wait_ge(sh["bc0"], 1)
            ve.tensor_copy(s_invb[:], ps_sc[:, 0:2]).then_inc(sh["invb0"], 1)

        @block.tensor
        def _(te):
            te.wait_ge(sh["ldc"], NCONSTS)
            for kind, i in FRONT:
                if kind == "L":
                    te.wait_ge(sh["xnb1"], i + 1)
                    for g4 in range(4):
                        G = 4 * i + g4
                        if G >= 2:
                            te.wait_ge(sh["ev1"], G - 1)
                        for c4 in range(4):
                            c = g4 * 4 + c4
                            ins = te.transpose(p1_tp[g4 % 2][:, c4 * 128:(c4 + 1) * 128],
                                               p1_xnb[:, c * 128:(c + 1) * 128], s_idbf[:])
                        ins.then_inc(sh["tp1"], 1)
                else:
                    te.wait_ge(sh[f"lc{i % 2}"], 16 * (i // 2 + 1))
                    for g4 in range(4):
                        G = 4 * i + g4
                        if G >= 2:
                            te.wait_ge(sh["ev0b"], G - 1)
                        for c4 in range(4):
                            c = g4 * 4 + c4
                            ins = te.transpose(ps_tp[g4 % 2][:, c4 * 128:(c4 + 1) * 128],
                                               p0_wc[i % 2][:, c * 128:(c + 1) * 128], s_idbf[:])
                        ins.then_inc(sh["tp0b"], 1)
            te.wait_ge(sh["vs0"], 1)
            te.matmul(ps_sc[0:1, 0:1], p0_vsum[:, 0:1], s_onecol[:],
                      start=True, stop=False, skip_group_check=True)
            te.matmul(ps_sc[0:1, 1:2], p0_vsum[:, 1:2], s_onecol[:],
                      start=False, stop=True, skip_group_check=True).then_inc(sh["mm1"], 1)
            te.wait_ge(sh["inv0"], 1)
            te.matmul(ps_sc[:, 0:2], s_onerow[:], p0_inv2[:],
                      start=True, stop=True, skip_group_check=True).then_inc(sh["bc0"], 1)

        def emit_quant_ve(ve, i):
            for hh in range(2):
                j = 2 * i + hh
                sl = slice(hh * (D // 2), (hh + 1) * (D // 2))
                ve.wait_ge(sh["qm"], j + 1)
                vfence(ve, ve.tensor_scalar(p0_qf[:], p0_qf[:],
                                            1.0, -1.0, ALU.min, ALU.max))
                ve.tensor_copy(p0_qi[:], p0_qf[:]).then_inc(sh["qh"], 1)
                ve.wait_ge(sh["qh"], j + 1)
                if hh == 0 and i >= 1:
                    ve.wait_ge(sh["tp0"], 4 * i)
                ins = ve.tensor_copy(p0_qb[:, sl], p0_qi[:])
                if hh == 0:
                    vfence(ve, ins)
                else:
                    ins.then_inc(sh["q0"], 1)
                    ve.wait_ge(sh["q0"], i + 1)

        @block.gpsimd
        def _(g):
            g.wait_ge(sh["st1"], 16)
            g.collective_compute("AllGather", ALU.bypass, replica_groups=GROUPS4,
                                 ins=[cc_xnh[0].opt()], outs=[g_xnh[0].opt()]).then_inc(sh["cc"], 1)
            g.wait_ge(sh["part0"], 1)
            g.dma_start(out=cc_sc_in, in_=p0_scp[:]).then_inc(sh["gd0"], 16)
            g.wait_ge(sh["gd0"], 16)
            g.collective_compute("AllReduce", ALU.add, replica_groups=GROUP8,
                                 ins=[cc_sc_in.opt()], outs=[cc_sc_out.opt()]).then_inc(sh["cc"], 1)
            g.wait_ge(sh["cc"], CC_AR)
            g.dma_start(out=p0_sct[:], in_=cc_sc_out).then_inc(sh["gd0"], 16)
            for i in range(2):
                g.dma_start(out=p0_w[i % 2][:], in_=w_src(i)).then_inc(sh[f"lq{i % 2}"], 16)
            g.wait_ge(sh["st1"], 32)
            g.collective_compute("AllGather", ALU.bypass, replica_groups=GROUPS4,
                                 ins=[cc_xnh[1].opt()], outs=[g_xnh[1].opt()]).then_inc(sh["cc"], 1)
            for i in range(16):
                if i >= 1:
                    g.wait_ge(sh["ev0"], 4 * i)
                    if i >= 2:
                        g.wait_ge(sh["sq"], 16 * (i - 1))
                    g.dma_start(out=w_dst(i - 1).rearrange("(c p) f -> p c f", p=128),
                                in_=p0_wt[:].rearrange("p (c f) -> p c f", c=16)).then_inc(sh["sq"], 16)
                if i + 2 < 16:
                    g.wait_ge(sh["qm"], 2 * (i + 1))
                    g.dma_start(out=p0_w[i % 2][:], in_=w_src(i + 2)).then_inc(sh[f"lq{i % 2}"], 16)
            g.wait_ge(sh["ev0"], 64)
            g.wait_ge(sh["sq"], 16 * 15)
            g.dma_start(out=w_dst(15).rearrange("(c p) f -> p c f", p=128),
                        in_=p0_wt[:].rearrange("p (c f) -> p c f", c=16)).then_inc(sh["sq"], 16)
            g.wait_ge(sh["sq"], 256)
            g.wait_ge(sh["stw0b"], 32)
            g.collective_compute("AllGather", ALU.bypass, replica_groups=GROUP8,
                                 ins=[cc_w1_in.opt()], outs=[g_w1qT.opt()]).then_inc(sh["cc"], 1)
            g.collective_compute("AllGather", ALU.bypass, replica_groups=GROUP8,
                                 ins=[cc_w2_in.opt()], outs=[g_w2qT.opt()]).then_inc(sh["cc"], 1)
            g.wait_ge(sh["zst3"], 32)
            g.collective_compute("AllGather", ALU.bypass, replica_groups=GROUPS4,
                                 ins=[cc_zh[0].opt()], outs=[g_zh[0].opt()]).then_inc(sh["cc"], 1)
            g.collective_compute("AllGather", ALU.bypass, replica_groups=GROUP8,
                                 ins=[cc_wo_in.opt()], outs=[g_woT.opt()]).then_inc(sh["cc"], 1)

        for _g in reversed(p1_tpg):
            _g.__exit__(None, None, None)
        p1.close()

        with contextlib.ExitStack() as p23:
            E23 = p23.enter_context
            s_qT = [E23(nc.sbuf_tensor(f"s_qT{u}", [128, S], BF16)) for u in range(4)]
            s_kT = [E23(nc.sbuf_tensor(f"s_kT{u}", [128, S], BF16)) for u in range(4)]
            s_v = [E23(nc.sbuf_tensor(f"s_v{u}", [128, NKC * 129], BF16)) for u in range(4)]

            with contextlib.ExitStack() as p2:
                E2 = p2.enter_context
                p2_sxn = [E2(nc.sbuf_tensor(f"p2_sxn{i}", [128, 16 * 512], BF16)) for i in range(2)]
                p2_swT = E2(nc.sbuf_tensor("p2_swT", [128, 4 * 2048 + 16 * 512], BF16))
                VOFF = 4 * 2048
                ps_qk = [E2(nc.psum_tensor(f"ps_qk{i}", [128, 512], F32)) for i in range(3)]
                ps_v = [E2(nc.psum_tensor(f"ps_v{i}", [128, 512], F32)) for i in range(2)]

                @block.vector
                def _(ve):
                    ve.wait_ge(sh["xnb1"], 8)
                    for u in range(4):
                        ve.memset(s_v[u][:], 1.0)

                P2ORDER = [(rk, h) for h in range(2) for rk in range(4)]

                for pas in range(2):
                    u0, u1 = 2 * pas, 2 * pas + 1

                    @block.sync
                    def _(sy, pas=pas, u0=u0, u1=u1):
                        if pas == 0:
                            sy.wait_ge(sh["stw0b"], 224)
                            sy.wait_ge(sh["xnb1"], 8)
                        if pas == 1:
                            sy.wait_ge(sh["mqk2"], 32)
                            sy.wait_ge(sh["mv2"], 32)
                        for k, (w, u) in enumerate([(0, u0), (0, u1), (1, u0), (1, u1)]):
                            src = l_wT[(w * 4 + u) * D:(w * 4 + u + 1) * D, :]
                            sy.dma_start(out=p2_swT[:, k * 2048:(k + 1) * 2048].rearrange("p (c f) -> p c f", c=16),
                                         in_=src.rearrange("(c p) f -> p c f", p=128)).then_inc(sh["wld2"], 16)
                        if pas == 0:
                            for u in range(4):
                                src = l_wT[(2 * 4 + u) * D:(2 * 4 + u + 1) * D, :]
                                dst = p2_swT[:, VOFF:].rearrange("p (c four f) -> p c four f", four=4, f=128)[:, :, u, :]
                                sy.dma_start(out=dst,
                                             in_=src.rearrange("(c p) f -> p c f", p=128)).then_inc(sh["wld2"], 16)
                        sy.wait_ge(sh["cc"], CC_XN0)
                        if pas == 0:
                            sy.wait_ge(sh["st1"], 32)
                        for e, (rk, h) in enumerate(P2ORDER):
                            if e >= 2:
                                if pas == 0:
                                    sy.wait_ge(sh["mv2"], 4 * (e - 1))
                                else:
                                    sy.wait_ge(sh["mqk2"], 32 + 4 * (e - 1))
                            if e == 4:
                                sy.wait_ge(sh["cc"], CC_XN1)
                            src = g_xnh[h][rk * D:(rk + 1) * D, :]
                            sy.dma_start(out=p2_sxn[e % 2][:].rearrange("p (c f) -> p c f", c=16),
                                         in_=src.rearrange("(c p) f -> p c f", p=128)).then_inc(sh[f"ln{e % 2}"], 16)

                    @block.tensor
                    def _(te, pas=pas):
                        if pas == 0:
                            te.wait_ge(sh["ev1"], 32)
                        te.wait_ge(sh["wld2"], 128 + 64 * pas)
                        for tb in range(8):
                            Tg = 8 * pas + tb
                            te.wait_ge(sh[f"ln{Tg % 2}"], 16 * (Tg // 2 + 1))
                            for k in range(4):
                                Gq = 4 * Tg + k
                                if Gq >= 3:
                                    te.wait_ge(sh["eqk2"], Gq - 2)
                                for c in range(16):
                                    ins = te.matmul(ps_qk[Gq % 3][:],
                                                    p2_swT[:, k * 2048 + c * 128: k * 2048 + (c + 1) * 128],
                                                    p2_sxn[tb % 2][:, c * 512:(c + 1) * 512],
                                                    start=(c == 0), stop=(c == 15),
                                                    skip_group_check=True)
                                ins.then_inc(sh["mqk2"], 1)
                            if pas == 0:
                                for ts in range(4):
                                    Gv = 4 * Tg + ts
                                    if Gv >= 2:
                                        te.wait_ge(sh["evv2"], Gv - 1)
                                    for c in range(16):
                                        ins = te.matmul(ps_v[ts % 2][:],
                                                        p2_sxn[tb % 2][:, c * 512 + ts * 128: c * 512 + ts * 128 + 128],
                                                        p2_swT[:, VOFF + c * 512: VOFF + (c + 1) * 512],
                                                        start=(c == 0), stop=(c == 15),
                                                        skip_group_check=True)
                                    ins.then_inc(sh["mv2"], 1)
                            te.wait_ge(sh["q0"], Tg + 1)
                            for g4 in range(4):
                                G = 4 * Tg + g4
                                if G >= 2:
                                    te.wait_ge(sh["ev0"], G - 1)
                                for c4 in range(4):
                                    c = g4 * 4 + c4
                                    ins = te.transpose(ps_tp[g4 % 2][:, c4 * 128:(c4 + 1) * 128],
                                                       p0_qb[:, c * 128:(c + 1) * 128], s_idbf[:])
                                ins.then_inc(sh["tp0"], 1)

                    @block.vector
                    def _(ve, pas=pas, u0=u0, u1=u1):
                        for e, (rk, h) in enumerate(P2ORDER):
                            Tg = 8 * pas + e
                            tb_val = 2 * rk + h
                            emit_quant_ve(ve, Tg)
                            for k, (dst, u) in enumerate([(s_qT, u0), (s_qT, u1), (s_kT, u0), (s_kT, u1)]):
                                Gq = 4 * Tg + k
                                ve.wait_ge(sh["mqk2"], Gq + 1)
                                ve.tensor_copy(dst[u][:, tb_val * 512:(tb_val + 1) * 512],
                                               ps_qk[Gq % 3][:]).then_inc(sh["eqk2"], 1)
                            if pas == 0:
                                for ts in range(4):
                                    Gv = 4 * Tg + ts
                                    ve.wait_ge(sh["mv2"], Gv + 1)
                                    kb = tb_val * 4 + ts
                                    for u in range(3):
                                        ve.tensor_copy(s_v[u][:, kb * 129: kb * 129 + 128],
                                                       ps_v[ts % 2][:, u * 128:(u + 1) * 128])
                                    ve.tensor_copy(s_v[3][:, kb * 129: kb * 129 + 128],
                                                   ps_v[ts % 2][:, 384:512]).then_inc(sh["evv2"], 1)
                            for g4 in range(4):
                                G = 4 * Tg + g4
                                ve.wait_ge(sh["tp0"], G + 1)
                                if Tg >= 1 and g4 == 0:
                                    ve.wait_ge(sh["sq"], 16 * Tg)
                                ve.tensor_copy(p0_wt[:, g4 * 512:(g4 + 1) * 512],
                                               ps_tp[g4 % 2][:, 0:512]).then_inc(sh["ev0"], 1)

            for _g in reversed(_wp_psg):
                _g.__exit__(None, None, None)

            with contextlib.ExitStack() as p3:
                E3 = p3.enter_context
                p3_ssm = [E3(nc.sbuf_tensor(f"p3_ssm{i}", [128, 512], F32)) for i in range(2)]
                p3_E = [E3(nc.sbuf_tensor(f"p3_E{i}", [128, 512], BF16)) for i in range(3)]
                p3_rl = E3(nc.sbuf_tensor("p3_rl", [128, 1], F32))
                p3_ztok = E3(nc.sbuf_tensor("p3_ztok", [128, 32 * 128], BF16))
                p3_zT = E3(nc.sbuf_tensor("p3_zT", [128, S], BF16))
                ps_s = [E3(nc.psum_tensor(f"ps_s{i}", [128, 512], F32)) for i in range(4)]
                ps_u = [E3(nc.psum_tensor(f"ps_u{i}", [128, 2 * 129], F32)) for i in range(2)]
                ps_zt = [E3(nc.psum_tensor(f"ps_zt{i}", [128, 1024], BF16)) for i in range(2)]

                def ps_u_col(q1):
                    return ps_u[q1 // 2][:, (q1 % 2) * 129:(q1 % 2) * 129 + 129]

                def tile_m(tl):
                    return tl["kc"] - 4 * tl["qb"] if tl["diag"] else 0

                @block.tensor
                def _(te):
                    te.wait_ge(sh["eqk2"], 64)
                    te.wait_ge(sh["evv2"], 32)
                    te.wait_ge(sh["ev0"], 64)

                    def emit_av(pv_idx, u):
                        pv = TILES[pv_idx]
                        m = tile_m(pv)
                        te.wait_ge(sh["e3"], pv_idx + 1)
                        if pv["first"]:
                            te.wait_ge(sh["z3"], QBIDX[pv_idx])
                        for q1 in range(m, 4):
                            stop = pv["diag"] and (m == q1)
                            ins = te.matmul(ps_u_col(q1),
                                            p3_E[pv_idx % 3][:, q1 * 128:(q1 + 1) * 128],
                                            s_v[u][:, pv["kc"] * 129: pv["kc"] * 129 + 129],
                                            start=pv["first"], stop=stop,
                                            skip_group_check=True)
                        ins.then_inc(sh["av3"], 1)

                    for u in range(4):
                        prev = None
                        for t in UNIT_T[u]:
                            tl = TILES[t]
                            if t >= 4:
                                t4 = TILES[t - 4]
                                if t4["diag"]:
                                    te.wait_ge(sh["c3"], NDIAG[t - 4])
                                else:
                                    te.wait_ge(sh["e3"], t - 3)
                            te.matmul(ps_s[t % 4][:],
                                      s_kT[u][:, tl["kc"] * 128:(tl["kc"] + 1) * 128],
                                      s_qT[u][:, tl["qb"] * QBW:(tl["qb"] + 1) * QBW],
                                      start=True, stop=True,
                                      skip_group_check=True).then_inc(sh["s3"], 1)
                            if prev is not None:
                                emit_av(prev, u)
                            prev = t
                        emit_av(prev, u)
                        te.wait_ge(sh["z3"], (u + 1) * NQB)
                        for g8 in range(8):
                            Gz = u * 8 + g8
                            if Gz >= 2:
                                te.wait_ge(sh["zev3"], Gz - 1)
                            for c4 in range(4):
                                blk = g8 * 4 + c4
                                ins = te.transpose(ps_zt[g8 % 2][:, c4 * 128:(c4 + 1) * 128],
                                                   p3_ztok[:, blk * 128:(blk + 1) * 128],
                                                   s_idbf[:])
                            ins.then_inc(sh["ztp3"], 1)

                @block.vector
                def _(ve):
                    ve.wait_ge(sh["sq"], 256)
                    for u in range(4):
                        for t in UNIT_T[u]:
                            tl = TILES[t]
                            if tl["diag"]:
                                d = NDIAG[t] - 1
                                m = tile_m(tl)
                                ve.wait_ge(sh["s3"], t + 1)
                                if d >= 2:
                                    ve.wait_ge(sh["e3"], DIAGS[d - 2] + 1)
                                ve.tensor_tensor(p3_ssm[d % 2][:, m * 128:512],
                                                 ps_s[t % 4][:, m * 128:512],
                                                 s_mask[:, m * 512 + m * 128:(m + 1) * 512],
                                                 ALU.add).then_inc(sh["c3"], 1)
                            if tl["last"]:
                                ve.wait_ge(sh["av3"], t + 1)
                                if tl["qb"] == 0 and u >= 1:
                                    ve.wait_ge(sh["ztp3"], u * 8)
                                for q1 in range(4):
                                    vfence(ve, ve.reciprocal(p3_rl[:], ps_u_col(q1)[:, 128:129]))
                                    blk = tl["qb"] * 4 + q1
                                    ins = ve.tensor_scalar(p3_ztok[:, blk * 128:(blk + 1) * 128],
                                                           ps_u_col(q1)[:, 0:128], p3_rl[:],
                                                           None, ALU.mult)
                                ins.then_inc(sh["z3"], 1)
                        for g8 in range(8):
                            Gz = u * 8 + g8
                            ve.wait_ge(sh["ztp3"], Gz + 1)
                            if g8 == 0 and u >= 1:
                                ve.wait_ge(sh["zst3"], 16 * u)
                            for c4 in range(4):
                                blk = g8 * 4 + c4
                                qb, q1 = blk // 4, blk % 4
                                ins = ve.tensor_copy(
                                    p3_zT[:, qb * 512 + q1 * 128: qb * 512 + (q1 + 1) * 128],
                                    ps_zt[g8 % 2][:, c4 * 128:(c4 + 1) * 128])
                            ins.then_inc(sh["zev3"], 1)

                @block.scalar
                def _(sc):
                    sc.wait_ge(sh["ldc"], NCONSTS)
                    for t in range(NT):
                        tl = TILES[t]
                        u = tl["u"]
                        fd = SLOT_FD[u]
                        m = tile_m(tl)
                        if tl["diag"]:
                            sc.wait_ge(sh["c3"], NDIAG[t])
                            src3 = p3_ssm[(NDIAG[t] - 1) % 2]
                            off = 0
                        else:
                            sc.wait_ge(sh["s3"], t + 1)
                            src3 = ps_s[t % 4]
                            off = 0
                        if t >= 3:
                            sc.wait_ge(sh["av3"], t - 2)
                        for qs in range(QBW // fd):
                            lo = max(qs * fd, m * 128)
                            hi = (qs + 1) * fd
                            if lo >= hi:
                                continue
                            ins = sc.activation(p3_E[t % 3][:, lo:hi],
                                                src3[:, lo:hi],
                                                AF.Exp,
                                                bias=s_btab[u][:, tl["col0"] + qs: tl["col0"] + qs + 1],
                                                scale=RS128)
                        ins.then_inc(sh["e3"], 1)

                @block.sync
                def _(sy):
                    for u in range(4):
                        sy.wait_ge(sh["zev3"], 8 * (u + 1))
                        if u >= 1:
                            sy.wait_ge(sh["zst3"], 16 * u)
                        sy.dma_start(out=cc_zh[u // 2][(u % 2) * 128:(u % 2) * 128 + 128, :],
                                     in_=p3_zT[:]).then_inc(sh["zst3"], 16)

        wprep.close()

        with contextlib.ExitStack() as p45:
            E4 = p45.enter_context
            p4_h = E4(nc.sbuf_tensor("p4_h", [128, 64 * 512], BF16))
            p4_xno = E4(nc.sbuf_tensor("p4_xno", [128, 16 * 512], BF16))
            p4_w1 = [E4(nc.sbuf_tensor(f"p4_w1_{i}", [128, 16 * 128], BF16)) for i in range(2)]
            p5_zT = E4(nc.sbuf_tensor("p5_zT", [128, 16 * 512], BF16))
            p5_w2 = [E4(nc.sbuf_tensor(f"p5_w2_{i}", [128, 64 * 128], BF16)) for i in range(2)]
            p5_wo = [E4(nc.sbuf_tensor(f"p5_wo_{i}", [128, 16 * 128], BF16)) for i in range(2)]
            p5_xr = E4(nc.sbuf_tensor("p5_xr", [128, 4 * D], F32))
            p5_sum = [E4(nc.sbuf_tensor(f"p5_sum{i}", [128, 512], F32)) for i in range(2)]
            p5_ob = [E4(nc.sbuf_tensor(f"p5_ob{i}", [128, 512], F32)) for i in range(2)]
            ps_h = [E4(nc.psum_tensor(f"ps_h{i}", [128, 512], F32)) for i in range(2)]
            ps_o = [E4(nc.psum_tensor(f"ps_o{i}", [128, 512], F32)) for i in range(2)]
            ps_t5 = [E4(nc.psum_tensor(f"ps_t5{i}", [128, 512], F32)) for i in range(2)]

            @block.gpsimd
            def _(g):
                g.wait_ge(sh["av3"], NT)
                for tb2 in range(2):
                    g.wait_ge(sh["cc"], CC_XN0 if tb2 == 0 else CC_XN1)
                    if tb2 == 1:
                        g.wait_ge(sh["mm4"], 64)
                    for c in range(16):
                        g.indirect_dma_start(
                            p4_xno[:, c * 512:(c + 1) * 512], None,
                            g_xnh[tb2][0:128, 0:512],
                            IndirectOffsetOnAxis(s_xnidx[:, tb2 * 16 + c: tb2 * 16 + c + 1], 1)
                        ).then_inc(sh["xg4"], 16)
                    if tb2 == 0:
                        g.wait_ge(sh["zst3"], 64)
                        g.collective_compute("AllGather", ALU.bypass, replica_groups=GROUPS4,
                                             ins=[cc_zh[1].opt()], outs=[g_zh[1].opt()]).then_inc(sh["cc"], 1)
                    if tb2 == 1:
                        g.wait_ge(sh["mm5"], 16)
                    for ec in range(16):
                        g.wait_ge(sh["cc"], CC_Z01 if ec < 8 else CC_Z23)
                        g.indirect_dma_start(
                            p5_zT[:, ec * 512:(ec + 1) * 512], None,
                            g_zh[ec // 8][0:128, 0:512],
                            IndirectOffsetOnAxis(s_zidx[:, tb2 * 16 + ec: tb2 * 16 + ec + 1], 1)
                        ).then_inc(sh["zg5"], 16)

            @block.sync
            def _(sy):
                sy.wait_ge(sh["av3"], NT)
                sy.wait_ge(sh["cc"], CC_W1)
                for tb2 in range(2):
                    for fc in range(64):
                        Fg = 64 * tb2 + fc
                        if Fg >= 2:
                            sy.wait_ge(sh["mm4"], Fg - 1)
                        rkf, fl = fc // 8, (fc % 8) * 128
                        src = g_w1qT[rkf * D:(rkf + 1) * D, fl:fl + 128]
                        sy.dma_start(out=p4_w1[fc % 2][:].rearrange("p (c f) -> p c f", c=16),
                                     in_=src.rearrange("(c p) f -> p c f", p=128)).then_inc(sh[f"l4{Fg % 2}"], 16)
                    if tb2 == 0:
                        sy.wait_ge(sh["ztp3"], 32)
                        sy.wait_ge(sh["zst3"], 64)
                        sy.wait_ge(sh["cc"], CC_WO)
                    if tb2 == 1:
                        sy.wait_ge(sh["os0"], 64 * 8)
                        sy.wait_ge(sh["os1"], 64 * 8)
                    for ts in range(4):
                        sy.dma_start(out=p5_xr[:, ts * D:(ts + 1) * D],
                                     in_=x_own[tb2 * 512 + ts * 128: tb2 * 512 + (ts + 1) * 128, :]).then_inc(sh["xl5"], 16)
                    for dc in range(16):
                        Dg = 16 * tb2 + dc
                        if Dg >= 2:
                            sy.wait_ge(sh["mm5"], Dg - 1)
                        rkd, dl = dc // 2, (dc % 2) * 128
                        src2 = g_w2qT[rkd * FF:(rkd + 1) * FF, dl:dl + 128]
                        sy.dma_start(out=p5_w2[dc % 2][:].rearrange("p (c f) -> p c f", c=64),
                                     in_=src2.rearrange("(c p) f -> p c f", p=128)).then_inc(sh[f"l2{Dg % 2}"], 16)
                        srco = g_woT[rkd * D:(rkd + 1) * D, dl:dl + 128]
                        sy.dma_start(out=p5_wo[dc % 2][:].rearrange("p (c f) -> p c f", c=16),
                                     in_=srco.rearrange("(c p) f -> p c f", p=128)).then_inc(sh[f"lo{Dg % 2}"], 16)
                        if dc >= 2:
                            Dg2 = Dg - 2
                            sy.wait_ge(sh["add5"], Dg2 + 1)
                            for ts in range(4):
                                sy.dma_start(out=out_own[tb2 * 512 + ts * 128: tb2 * 512 + (ts + 1) * 128,
                                                         (dc - 2) * 128:(dc - 1) * 128],
                                             in_=p5_ob[Dg2 % 2][:, ts * 128:(ts + 1) * 128]).then_inc(sh[f"os{Dg2 % 2}"], 16)
                    for dc in (14, 15):
                        Dg2 = 16 * tb2 + dc
                        sy.wait_ge(sh["add5"], Dg2 + 1)
                        for ts in range(4):
                            sy.dma_start(out=out_own[tb2 * 512 + ts * 128: tb2 * 512 + (ts + 1) * 128,
                                                     dc * 128:(dc + 1) * 128],
                                         in_=p5_ob[Dg2 % 2][:, ts * 128:(ts + 1) * 128]).then_inc(sh[f"os{Dg2 % 2}"], 16)
                sy.wait_ge(sh["os0"], 64 * 16)
                sy.wait_ge(sh["os1"], 64 * 16)
                if _DEBUG:
                    sy.dma_start(out=d_gxn[0:4 * D, 0:512], in_=g_xnh[0]).then_inc(sh["os0"], 16)
                    sy.dma_start(out=d_gz[0:1024, :], in_=g_zh[0]).then_inc(sh["os0"], 16)
                    sy.dma_start(out=d_w1, in_=cc_w1_in).then_inc(sh["os0"], 16)
                    sy.dma_start(out=d_sc, in_=cc_sc_out).then_inc(sh["os0"], 16)
                    sy.wait_ge(sh["os0"], 64 * 16 + 64)

            @block.tensor
            def _(te):
                for tb2 in range(2):
                    te.wait_ge(sh["xg4"], 256 * (tb2 + 1))
                    for fc in range(64):
                        Fg = 64 * tb2 + fc
                        te.wait_ge(sh[f"l4{Fg % 2}"], 16 * (Fg // 2 + 1))
                        if Fg >= 2:
                            te.wait_ge(sh["gel4"], Fg - 1)
                        for c in range(16):
                            ins = te.matmul(ps_h[fc % 2][:],
                                            p4_w1[fc % 2][:, c * 128:(c + 1) * 128],
                                            p4_xno[:, c * 512:(c + 1) * 512],
                                            start=(c == 0), stop=(c == 15),
                                            skip_group_check=True)
                        ins.then_inc(sh["mm4"], 1)
                    te.wait_ge(sh["zg5"], 256 * (tb2 + 1))
                    te.wait_ge(sh["gel4"], 64 * (tb2 + 1))
                    for dc in range(16):
                        Dg = 16 * tb2 + dc
                        te.wait_ge(sh[f"l2{Dg % 2}"], 16 * (Dg // 2 + 1))
                        te.wait_ge(sh[f"lo{Dg % 2}"], 16 * (Dg // 2 + 1))
                        if Dg >= 2:
                            te.wait_ge(sh["sc5"], Dg - 1)
                        for fc in range(64):
                            te.matmul(ps_o[dc % 2][:],
                                      p5_w2[dc % 2][:, fc * 128:(fc + 1) * 128],
                                      p4_h[:, fc * 512:(fc + 1) * 512],
                                      start=(fc == 0), stop=False,
                                      skip_group_check=True)
                        for ec in range(16):
                            ins = te.matmul(ps_o[dc % 2][:],
                                            p5_wo[dc % 2][:, ec * 128:(ec + 1) * 128],
                                            p5_zT[:, ec * 512:(ec + 1) * 512],
                                            start=False, stop=(ec == 15),
                                            skip_group_check=True)
                        ins.then_inc(sh["mm5"], 1)
                        te.wait_ge(sh["sc5"], Dg + 1)
                        if Dg >= 2:
                            te.wait_ge(sh["add5"], Dg - 1)
                        for ts in range(4):
                            ins = te.transpose(ps_t5[dc % 2][:, ts * 128:(ts + 1) * 128],
                                               p5_sum[dc % 2][:, ts * 128:(ts + 1) * 128],
                                               s_idf32[:])
                        ins.then_inc(sh["tp5"], 1)

            @block.scalar
            def _(sc):
                for tb2 in range(2):
                    for fc in range(64):
                        Fg = 64 * tb2 + fc
                        sc.wait_ge(sh["mm4"], Fg + 1)
                        if tb2 == 1 and fc == 0:
                            sc.wait_ge(sh["mm5"], 16)
                        sc.activation(p4_h[:, fc * 512:(fc + 1) * 512], ps_h[fc % 2][:],
                                      AF.Gelu, bias=s_b1[:, fc:fc + 1],
                                      scale=1.0).then_inc(sh["gel4"], 1)
                    for dc in range(16):
                        Dg = 16 * tb2 + dc
                        sc.wait_ge(sh["mm5"], Dg + 1)
                        if Dg >= 2:
                            sc.wait_ge(sh["tp5"], Dg - 1)
                        sc.activation(p5_sum[dc % 2][:], ps_o[dc % 2][:], AF.Identity,
                                      bias=s_b2[:, dc:dc + 1],
                                      scale=1.0).then_inc(sh["sc5"], 1)

            @block.vector
            def _(ve):
                for tb2 in range(2):
                    for dc in range(16):
                        Dg = 16 * tb2 + dc
                        ve.wait_ge(sh["tp5"], Dg + 1)
                        if dc == 0:
                            ve.wait_ge(sh["xl5"], 64 * (tb2 + 1))
                        if Dg >= 2:
                            ve.wait_ge(sh[f"os{Dg % 2}"], 64 * (Dg // 2))
                        for ts in range(4):
                            ins = ve.tensor_tensor(p5_ob[dc % 2][:, ts * 128:(ts + 1) * 128],
                                                   p5_xr[:, ts * D + dc * 128: ts * D + (dc + 1) * 128],
                                                   ps_t5[dc % 2][:, ts * 128:(ts + 1) * 128],
                                                   ALU.add)
                        ins.then_inc(sh["add5"], 1)

    return nc



def _stage_inputs(L0_attn_resid_pre, L0_attn_Wq, L0_attn_Wk, L0_attn_Wv,
                  L0_attn_Wo, L0_ffn_w1, L0_ffn_b1, L0_ffn_w2, L0_ffn_b2):
    x2d = np.ascontiguousarray(np.asarray(L0_attn_resid_pre, dtype=np.float32).reshape(T, D))
    Wq = np.ascontiguousarray(np.asarray(L0_attn_Wq, dtype=np.float32))
    Wk = np.ascontiguousarray(np.asarray(L0_attn_Wk, dtype=np.float32))
    Wv = np.ascontiguousarray(np.asarray(L0_attn_Wv, dtype=np.float32))
    Wo = np.ascontiguousarray(np.asarray(L0_attn_Wo, dtype=np.float32))
    w1 = np.ascontiguousarray(np.asarray(L0_ffn_w1, dtype=np.float32))
    b1 = np.ascontiguousarray(np.asarray(L0_ffn_b1, dtype=np.float32))
    w2 = np.ascontiguousarray(np.asarray(L0_ffn_w2, dtype=np.float32))
    b2 = np.ascontiguousarray(np.asarray(L0_ffn_b2, dtype=np.float32))
    id_bf = np.eye(128, dtype=ml_dtypes.bfloat16)
    id_f32 = np.eye(128, dtype=np.float32)

    in_maps = []
    for r in range(8):
        heads = [s * 4 + (r % 4) for s in range(4)]

        def hrows(W):
            return np.ascontiguousarray(
                np.concatenate([W[h * 128:(h + 1) * 128, :] for h in heads], axis=0))

        btab = {f"btab{s}": build_bias_table(heads[s]) for s in range(4)}
        masks = np.zeros((128, 4 * 512), dtype=np.float32)
        jp_ = np.arange(128)[:, None]
        qf_ = np.arange(512)[None, :]
        for m in range(4):
            masks[:, m * 512:(m + 1) * 512] = np.where(jp_ + m * 128 > qf_, NEGBIG, 0.0)
        xnidx = np.zeros((128, 32), dtype=np.int32)
        zidx = np.zeros((128, 32), dtype=np.int32)
        p = np.arange(128)
        for tb2 in range(2):
            for c in range(16):
                rows = (r % 4) * D + c * 128 + p
                xnidx[:, tb2 * 16 + c] = rows * 512
            for ec in range(16):
                rows = (ec % 4) * 256 + ((ec // 4) % 2) * 128 + p
                zidx[:, tb2 * 16 + ec] = rows * S + (r % 4) * 1024 + tb2 * 512
        in_maps.append(dict(
            x_own=x2d[r * TC:(r + 1) * TC],
            wq_u=hrows(Wq), wk_u=hrows(Wk), wv_u=hrows(Wv),
            wo_own=np.ascontiguousarray(Wo[r * 256:(r + 1) * 256, :]),
            w1_own=np.ascontiguousarray(w1[r * 1024:(r + 1) * 1024, :]),
            w2_own=np.ascontiguousarray(w2[r * 256:(r + 1) * 256, :]),
            b1_in=np.ascontiguousarray(b1.reshape(FF // 128, 128).T), b2_in=np.ascontiguousarray(b2.reshape(D // 128, 128).T), id_bf=id_bf, id_f32=id_f32,
            zidx=zidx, xnidx=xnidx, masks=masks, **btab,
        ))
    return in_maps


_NC_CACHE = []


def kernel(**inputs):
    in_maps = _stage_inputs(**inputs)
    if not _NC_CACHE:
        _NC_CACHE.append(build_program())
    nc = _NC_CACHE[0]
    res = run_bass_kernel_spmd(nc, in_maps, core_ids=list(range(8)))
    out = np.concatenate([res.results[r]["out_own"] for r in range(8)], axis=0)
    return np.ascontiguousarray(out.reshape(B, S, D)).astype(np.float32)


# version 22 (final)
# speedup vs baseline: 1.0543x; 1.0543x over previous
import contextlib
import math

import numpy as np
import ml_dtypes

import concourse.bass as bass
import concourse.mybir as mybir
from concourse.bass import IndirectOffsetOnAxis
from concourse.bass_utils import run_bass_kernel_spmd

F32 = mybir.dt.float32
BF16 = mybir.dt.bfloat16
I32 = mybir.dt.int32
AF = mybir.ActivationFunctionType
ALU = mybir.AluOpType
AX = mybir.AxisListType

B, S, D, H, DH = 2, 4096, 2048, 16, 128
FF = 8192
T = B * S
TC = T // 8
GROUPS4 = [[0, 1, 2, 3], [4, 5, 6, 7]]
GROUP8 = [list(range(8))]
RS128 = float(np.float32(1.0 / math.sqrt(128.0)))
CLIPRAW = float(np.float32(1024.0 * math.sqrt(128.0)))
NEGBIG = -1.0e30
QBW = 512
NQB = S // QBW
NKC = S // 128

SLOT_FD = [64, 256, 512, 512]
SLOT_WIN = [832, None, None, None]


def _alibi_slopes(n):
    def pow2(m):
        start = 2 ** (-(2 ** (-(math.log2(m) - 3))))
        return [start * start ** i for i in range(m)]
    if math.log2(n).is_integer():
        return pow2(n)
    c = 2 ** math.floor(math.log2(n))
    return pow2(c) + _alibi_slopes(2 * c)[0::2][: n - c]


SLOPES = np.array(_alibi_slopes(H), dtype=np.float32)


def kc_range(s, qb):
    hi = 4 * qb + 4
    w = SLOT_WIN[s]
    lo = 0 if w is None else max(0, (QBW * qb - w) // 128)
    return list(range(lo, hi))


def slot_cols(s):
    fd = SLOT_FD[s]
    return [128 * kc - (QBW * qb + qs * fd)
            for qb in range(NQB) for kc in kc_range(s, qb)
            for qs in range(QBW // fd)]


def build_bias_table(head):
    sig = SLOPES[head]
    offs = np.array(slot_cols(head // 4), dtype=np.float32)
    jp = np.arange(128, dtype=np.float32)[:, None]
    return (np.float32(sig) * (jp + offs[None, :])).astype(np.float32)


def build_tiles():
    tiles = []
    for u in range(4):
        col = 0
        for qb in range(NQB):
            kcs = kc_range(u, qb)
            for kc in kcs:
                tiles.append(dict(u=u, qb=qb, kc=kc,
                                  first=(kc == kcs[0]), last=(kc == kcs[-1]),
                                  diag=(kc >= 4 * qb), col0=col))
                col += QBW // SLOT_FD[u]
    return tiles


TILES = build_tiles()
NT = len(TILES)
NDIAG = []
_nd = 0
for _t in TILES:
    if _t["diag"]:
        _nd += 1
    NDIAG.append(_nd)
QBIDX = [t["u"] * NQB + t["qb"] for t in TILES]
UNIT_T = [[i for i, t in enumerate(TILES) if t["u"] == u] for u in range(4)]
DIAGS = [i for i, t in enumerate(TILES) if t["diag"]]


_DEBUG = False


def build_program():
    nc = bass.Bass("TRN2", target_bir_lowering=False)

    x_own = nc.dram_tensor("x_own", [TC, D], F32, kind="ExternalInput").ap()
    wq_u = nc.dram_tensor("wq_u", [4 * 128, D], F32, kind="ExternalInput").ap()
    wk_u = nc.dram_tensor("wk_u", [4 * 128, D], F32, kind="ExternalInput").ap()
    wv_u = nc.dram_tensor("wv_u", [4 * 128, D], F32, kind="ExternalInput").ap()
    wo_own = nc.dram_tensor("wo_own", [256, D], F32, kind="ExternalInput").ap()
    w1_own = nc.dram_tensor("w1_own", [1024, D], F32, kind="ExternalInput").ap()
    w2_own = nc.dram_tensor("w2_own", [256, FF], F32, kind="ExternalInput").ap()
    b1_in = nc.dram_tensor("b1_in", [128, FF // 128], F32, kind="ExternalInput").ap()
    b2_in = nc.dram_tensor("b2_in", [128, D // 128], F32, kind="ExternalInput").ap()
    id_bf = nc.dram_tensor("id_bf", [128, 128], BF16, kind="ExternalInput").ap()
    id_f32 = nc.dram_tensor("id_f32", [128, 128], F32, kind="ExternalInput").ap()
    btabs_in = [nc.dram_tensor(f"btab{s}", [128, len(slot_cols(s))], F32,
                               kind="ExternalInput").ap() for s in range(4)]
    masks_in = nc.dram_tensor("masks", [128, 4 * 512], F32, kind="ExternalInput").ap()
    zidx_in = nc.dram_tensor("zidx", [128, 32], I32, kind="ExternalInput").ap()
    xnidx_in = nc.dram_tensor("xnidx", [128, 32], I32, kind="ExternalInput").ap()
    out_own = nc.dram_tensor("out_own", [TC, D], F32, kind="ExternalOutput").ap()

    cc_sc_in = nc.dram_tensor("cc_sc_in", [1, 2], F32, kind="Internal").ap()
    cc_sc_out = nc.dram_tensor("cc_sc_out", [1, 2], F32, kind="Internal").ap()
    cc_w1_in = nc.dram_tensor("cc_w1_in", [D, 1024], BF16, kind="Internal").ap()
    g_w1qT = nc.dram_tensor("g_w1qT", [8 * D, 1024], BF16, kind="Internal").ap()
    cc_w2_in = nc.dram_tensor("cc_w2_in", [FF, 256], BF16, kind="Internal").ap()
    g_w2qT = nc.dram_tensor("g_w2qT", [8 * FF, 256], BF16, kind="Internal").ap()
    cc_wo_in = nc.dram_tensor("cc_wo_in", [D, 256], BF16, kind="Internal").ap()
    g_woT = nc.dram_tensor("g_woT", [8 * D, 256], BF16, kind="Internal").ap()
    cc_xnh = [nc.dram_tensor(f"cc_xnh{h}", [D, 512], BF16, kind="Internal").ap()
              for h in range(2)]
    g_xnh = [nc.dram_tensor(f"g_xnh{h}", [4 * D, 512], BF16, kind="Internal").ap()
             for h in range(2)]
    cc_zh = [nc.dram_tensor(f"cc_zh{h}", [256, S], BF16, kind="Internal").ap()
             for h in range(2)]
    g_zh = [nc.dram_tensor(f"g_zh{h}", [1024, S], BF16, kind="Internal").ap()
            for h in range(2)]
    l_wT = nc.dram_tensor("l_wT", [12 * D, 128], BF16, kind="Internal").ap()
    if _DEBUG:
        d_gxn = nc.dram_tensor("d_gxn", [4 * D, TC], BF16, kind="ExternalOutput").ap()
        d_gz = nc.dram_tensor("d_gz", [D, S], BF16, kind="ExternalOutput").ap()
        d_w1 = nc.dram_tensor("d_w1", [D, 1024], BF16, kind="ExternalOutput").ap()
        d_sc = nc.dram_tensor("d_sc", [1, 2], F32, kind="ExternalOutput").ap()

    with contextlib.ExitStack() as ctx:
        E = ctx.enter_context
        SEMS = [
            "ldc",
            "la0", "la1", "abs0", "vs0", "mm1", "part0", "gd0", "inv0", "bc0", "invb0",
            "lq0", "lq1", "q0", "qm", "qh", "tp0", "ev0", "sq",
            "lc0", "lc1", "tp0b", "ev0b", "stw0b",
            "lx0", "lx1", "lx2", "lx3", "lx4", "lx5", "lx6", "lx7", "stats1", "var1", "std1", "rstd1", "xnb1", "tp1", "ev1", "st1",
            "wld2", "ln0", "ln1", "mqk2", "mv2", "eqk2", "evv2",
            "s3", "c3", "g3", "e3", "av3", "z3", "ztp3", "zev3", "zst3",
            "l40", "l41", "xg4", "mm4", "gel4",
            "xl5", "l20", "l21", "lo0", "lo1", "zg5", "mm5", "sc5", "tp5", "add5",
            "os0", "os1",
            "cc", "vf", "gf", "scf",
        ]
        sh = {nm: E(nc.semaphore(nm)) for nm in SEMS}
        block = E(nc.Block())
        _vf = [0]
        _gf = [0]

        def vfence(eng, instr):
            instr.then_inc(sh["vf"], 1)
            _vf[0] += 1
            eng.wait_ge(sh["vf"], _vf[0])

        def gfence(eng, instr):
            instr.then_inc(sh["gf"], 1)
            _gf[0] += 1
            eng.wait_ge(sh["gf"], _gf[0])

        CC_XN0, CC_AR, CC_XN1, CC_W1, CC_W2, CC_Z01, CC_WO, CC_Z23 = \
            1, 2, 3, 4, 5, 6, 7, 8

        s_idbf = E(nc.sbuf_tensor("s_idbf", [128, 128], BF16))
        s_idf32 = E(nc.sbuf_tensor("s_idf32", [128, 128], F32))
        s_btab = [E(nc.sbuf_tensor(f"s_btab{s}", [128, len(slot_cols(s))], F32))
                  for s in range(4)]
        s_b1 = E(nc.sbuf_tensor("s_b1", [128, FF // 128], F32))
        s_b2 = E(nc.sbuf_tensor("s_b2", [128, D // 128], F32))
        s_mask = E(nc.sbuf_tensor("s_mask", [128, 4 * 512], F32))
        s_zidx = E(nc.sbuf_tensor("s_zidx", [128, 32], I32))
        s_xnidx = E(nc.sbuf_tensor("s_xnidx", [128, 32], I32))
        s_invb = E(nc.sbuf_tensor("s_invb", [128, 2], F32))
        s_onecol = E(nc.sbuf_tensor("s_onecol", [128, 1], F32))
        s_onerow = E(nc.sbuf_tensor("s_onerow", [1, 128], F32))

        @block.sync
        def _(sy):
            sy.dma_start(out=s_idbf[:], in_=id_bf).then_inc(sh["ldc"], 16)
            sy.dma_start(out=s_idf32[:], in_=id_f32).then_inc(sh["ldc"], 16)
            for s in range(4):
                sy.dma_start(out=s_btab[s][:], in_=btabs_in[s]).then_inc(sh["ldc"], 16)
            sy.dma_start(out=s_b1[:], in_=b1_in).then_inc(sh["ldc"], 16)
            sy.dma_start(out=s_b2[:], in_=b2_in).then_inc(sh["ldc"], 16)
            sy.dma_start(out=s_mask[:], in_=masks_in).then_inc(sh["ldc"], 16)
            sy.dma_start(out=s_zidx[:], in_=zidx_in).then_inc(sh["ldc"], 16)
            sy.dma_start(out=s_xnidx[:], in_=xnidx_in).then_inc(sh["ldc"], 16)
        NCONSTS = 16 * 11

        wprep = contextlib.ExitStack()
        E0 = wprep.enter_context
        p0_w = [E0(nc.sbuf_tensor(f"p0_w{i}", [128, D], F32)) for i in range(2)]
        p0_absco = E0(nc.sbuf_tensor("p0_absco", [128, 16], F32))
        p0_vsum = E0(nc.sbuf_tensor("p0_vsum", [128, 2], F32))
        p0_scp = E0(nc.sbuf_tensor("p0_scp", [1, 2], F32))
        p0_sct = E0(nc.sbuf_tensor("p0_sct", [1, 2], F32))
        p0_inv2 = E0(nc.sbuf_tensor("p0_inv2", [1, 2], F32))
        p0_qf = E0(nc.sbuf_tensor("p0_qf", [128, D // 2], F32))
        p0_qi = E0(nc.sbuf_tensor("p0_qi", [128, D // 2], I32))
        p0_qb = E0(nc.sbuf_tensor("p0_qb", [128, D], BF16))
        p0_wt = E0(nc.sbuf_tensor("p0_wt", [128, D], BF16))
        _wp_psg = [nc.psum_tensor(f"ps_tp{i}", [128, 1024], BF16) for i in range(2)]
        _wp_psg.append(nc.psum_tensor("ps_sc", [128, 512], F32))
        ps_tp = [_wp_psg[0].__enter__(), _wp_psg[1].__enter__()]
        ps_sc = _wp_psg[2].__enter__()

        def w_src(i):
            if i < 8:
                return w1_own[i * 128:(i + 1) * 128, :]
            j = i - 8
            r2, c2 = j // 4, j % 4
            return w2_own[r2 * 128:(r2 + 1) * 128, c2 * D:(c2 + 1) * D]

        def w_dst(i):
            if i < 8:
                return cc_w1_in[:, i * 128:(i + 1) * 128]
            j = i - 8
            r2, c2 = j // 4, j % 4
            return cc_w2_in[c2 * D:(c2 + 1) * D, r2 * 128:(r2 + 1) * 128]

        def wc_src(i):
            if i < 2:
                return wo_own[i * 128:(i + 1) * 128, :]
            w, u = (i - 2) // 4, (i - 2) % 4
            return [wq_u, wk_u, wv_u][w][u * 128:(u + 1) * 128, :]

        p1 = contextlib.ExitStack()
        E1 = p1.enter_context
        p0_wc = [E1(nc.sbuf_tensor(f"p0_wc{i}", [128, D], BF16)) for i in range(2)]

        @block.gpsimd
        def _(g):
            for i in range(14):
                if i >= 2:
                    g.wait_ge(sh["tp0b"], 4 * (i - 1))
                g.dma_start(out=p0_wc[i % 2][:], in_=wc_src(i)).then_inc(sh[f"lc{i % 2}"], 16)

        FRONT = []
        _ci = 0
        for _li in range(8):
            FRONT.append(("L", _li))
            for _ in range(2):
                if _ci < 14:
                    FRONT.append(("C", _ci))
                    _ci += 1
        while _ci < 14:
            FRONT.append(("C", _ci))
            _ci += 1

        p1_x = [E1(nc.sbuf_tensor(f"p1_x{i}", [128, D], F32)) for i in range(8)]
        p1_scr = E1(nc.sbuf_tensor("p1_scr", [128, D], F32))
        p1_xnb = E1(nc.sbuf_tensor("p1_xnb", [128, D], BF16))
        p1_sums = E1(nc.sbuf_tensor("p1_sums", [128, 2], F32))
        p1_mean = E1(nc.sbuf_tensor("p1_mean", [128, 1], F32))
        p1_var = E1(nc.sbuf_tensor("p1_var", [128, 1], F32))
        p1_std = E1(nc.sbuf_tensor("p1_std", [128, 1], F32))
        p1_rstd = E1(nc.sbuf_tensor("p1_rstd", [128, 1], F32))
        p1_nmr = E1(nc.sbuf_tensor("p1_nmr", [128, 1], F32))
        p1_tmp = E1(nc.sbuf_tensor("p1_tmp", [128, 1], F32))
        p1_m2 = E1(nc.sbuf_tensor("p1_m2", [128, 1], F32))
        s_xnT = E1(nc.sbuf_tensor("s_xnT", [128, 16 * TC], BF16))
        p1_tpg = [nc.psum_tensor(f"p1_tp{i}", [128, 1024], BF16) for i in range(2)]
        p1_tp = [g.__enter__() for g in p1_tpg]

        @block.sync
        def _(sy):
            for i in range(8):
                sy.dma_start(out=p1_x[i][:], in_=x_own[i * 128:(i + 1) * 128, :]).then_inc(sh[f"lx{i}"], 16)
            for i in range(14):
                sy.wait_ge(sh["ev0b"], 4 * (i + 1))
                if i >= 1:
                    sy.wait_ge(sh["stw0b"], 16 * i)
                if i < 2:
                    dst = cc_wo_in[:, i * 128:(i + 1) * 128]
                else:
                    dst = l_wT[(i - 2) * D:(i - 1) * D, :]
                sy.dma_start(out=dst.rearrange("(c p) f -> p c f", p=128),
                             in_=p0_wt[:].rearrange("p (c f) -> p c f", c=16)).then_inc(sh["stw0b"], 16)
            for h in range(2):
                sy.wait_ge(sh["ev1"], 16 * (h + 1))
                if h >= 1:
                    sy.wait_ge(sh["st1"], 16 * h)
                sy.dma_start(out=cc_xnh[h].rearrange("(c p) t -> p c t", p=128),
                             in_=s_xnT[:].rearrange("p (c t) -> p c t", c=16)[:, :, h * 512:(h + 1) * 512]).then_inc(sh["st1"], 16)

        @block.scalar
        def _(sc):
            for i in range(2):
                sc.dma_start(out=p0_w[i % 2][:], in_=w_src(i)).then_inc(sh[f"la{i % 2}"], 16)
            def emit_abs(t):
                sc.wait_ge(sh[f"la{t % 2}"], 16 * (t // 2 + 1))
                if t % 2 == 0:
                    sc.wait_ge(sh["stats1"], t // 2 + 1)
                if t >= 1:
                    sc.wait_ge(sh["abs0"], t)
                sc.activation(p1_scr[:], p0_w[t % 2][:], AF.Abs,
                              accum_out=p0_absco[:, t:t + 1]).then_inc(sh["abs0"], 1)
                if t + 2 < 16:
                    sc.wait_ge(sh["abs0"], t + 1)
                    sc.dma_start(out=p0_w[t % 2][:], in_=w_src(t + 2)).then_inc(sh[f"la{t % 2}"], 16)
            for i in range(8):
                sc.wait_ge(sh[f"lx{i}"], 16)
                if i >= 1:
                    sc.wait_ge(sh["var1"], i)
                    sc.wait_ge(sh["abs0"], 2 * i)
                sc.activation(p1_scr[:], p1_x[i][:], AF.Identity,
                              accum_out=p1_sums[:, 0:1]).then_inc(sh["scf"], 1)
                sc.wait_ge(sh["scf"], i + 1)
                sc.activation(p1_scr[:], p1_x[i][:], AF.Square,
                              accum_out=p1_sums[:, 1:2]).then_inc(sh["stats1"], 1)
                emit_abs(2 * i)
                sc.wait_ge(sh["var1"], i + 1)
                sc.activation(p1_std[:], p1_var[:], AF.Sqrt).then_inc(sh["std1"], 1)
                emit_abs(2 * i + 1)
                sc.wait_ge(sh["rstd1"], i + 1)
                if i >= 1:
                    sc.wait_ge(sh["tp1"], 4 * i)
                sc.activation(p1_xnb[:], p1_x[i][:], AF.Identity,
                              bias=p1_nmr[:], scale=p1_rstd[:]).then_inc(sh["xnb1"], 1)
            for i in range(16):
                for hh in range(2):
                    j = 2 * i + hh
                    if j == 0:
                        sc.wait_ge(sh["invb0"], 1)
                    sc.wait_ge(sh[f"lq{i % 2}"], 16 * (i // 2 + 1))
                    if j >= 1:
                        sc.wait_ge(sh["qh"], j)
                    m = 0 if i < 8 else 1
                    sl = slice(hh * (D // 2), (hh + 1) * (D // 2))
                    sc.activation(p0_qf[:], p0_w[i % 2][:, sl], AF.Identity,
                                  scale=s_invb[:, m:m + 1]).then_inc(sh["qm"], 1)

        @block.vector
        def _(ve):
            for kind, i in FRONT:
                if kind == "C":
                    for g4 in range(4):
                        G = 4 * i + g4
                        ve.wait_ge(sh["tp0b"], G + 1)
                        if i >= 1 and g4 == 0:
                            ve.wait_ge(sh["stw0b"], 16 * i)
                        ve.tensor_copy(p0_wt[:, g4 * 512:(g4 + 1) * 512],
                                       ps_tp[g4 % 2][:, 0:512]).then_inc(sh["ev0b"], 1)
                    continue
                ve.wait_ge(sh["stats1"], i + 1)
                ve.tensor_scalar(p1_mean[:], p1_sums[:, 0:1], 1.0 / D, None, ALU.mult)
                vfence(ve, ve.tensor_scalar(p1_tmp[:], p1_sums[:, 1:2], 1.0 / D, None, ALU.mult))
                vfence(ve, ve.tensor_tensor(p1_m2[:], p1_mean[:], p1_mean[:], ALU.mult))
                vfence(ve, ve.tensor_sub(p1_var[:], p1_tmp[:], p1_m2[:]))
                ve.tensor_scalar(p1_var[:], p1_var[:], 1e-5, None, ALU.add).then_inc(sh["var1"], 1)
                ve.wait_ge(sh["std1"], i + 1)
                vfence(ve, ve.reciprocal(p1_rstd[:], p1_std[:]))
                ve.scalar_tensor_tensor(p1_nmr[:], p1_mean[:], -1.0, p1_rstd[:],
                                        ALU.mult, ALU.mult).then_inc(sh["rstd1"], 1)
                for g4 in range(4):
                    G = 4 * i + g4
                    ve.wait_ge(sh["tp1"], G + 1)
                    for c4 in range(4):
                        c = g4 * 4 + c4
                        ins = ve.tensor_copy(
                            s_xnT[:, c * TC + i * 128: c * TC + (i + 1) * 128],
                            p1_tp[g4 % 2][:, c4 * 128:(c4 + 1) * 128])
                    ins.then_inc(sh["ev1"], 1)
            ve.wait_ge(sh["abs0"], 16)
            ve.memset(s_onecol[:], 1.0)
            ve.memset(s_onerow[:], 1.0)
            ve.reduce_sum(p0_vsum[:, 0:1], p0_absco[:, 0:8], axis=AX.X)
            ve.reduce_sum(p0_vsum[:, 1:2], p0_absco[:, 8:16], axis=AX.X).then_inc(sh["vs0"], 1)
            ve.wait_ge(sh["mm1"], 1)
            ve.tensor_copy(p0_scp[:], ps_sc[0:1, 0:2]).then_inc(sh["part0"], 1)
            ve.wait_ge(sh["gd0"], 32)
            vfence(ve, ve.tensor_scalar(p0_sct[:], p0_sct[:], 1.0 / float(FF * D), 1e-8,
                                        ALU.mult, ALU.max))
            ve.reciprocal(p0_inv2[:], p0_sct[:]).then_inc(sh["inv0"], 1)
            ve.wait_ge(sh["bc0"], 1)
            ve.tensor_copy(s_invb[:], ps_sc[:, 0:2]).then_inc(sh["invb0"], 1)

        @block.tensor
        def _(te):
            te.wait_ge(sh["ldc"], NCONSTS)
            for kind, i in FRONT:
                if kind == "L":
                    te.wait_ge(sh["xnb1"], i + 1)
                    for g4 in range(4):
                        G = 4 * i + g4
                        if G >= 2:
                            te.wait_ge(sh["ev1"], G - 1)
                        for c4 in range(4):
                            c = g4 * 4 + c4
                            ins = te.transpose(p1_tp[g4 % 2][:, c4 * 128:(c4 + 1) * 128],
                                               p1_xnb[:, c * 128:(c + 1) * 128], s_idbf[:])
                        ins.then_inc(sh["tp1"], 1)
                else:
                    te.wait_ge(sh[f"lc{i % 2}"], 16 * (i // 2 + 1))
                    for g4 in range(4):
                        G = 4 * i + g4
                        if G >= 2:
                            te.wait_ge(sh["ev0b"], G - 1)
                        for c4 in range(4):
                            c = g4 * 4 + c4
                            ins = te.transpose(ps_tp[g4 % 2][:, c4 * 128:(c4 + 1) * 128],
                                               p0_wc[i % 2][:, c * 128:(c + 1) * 128], s_idbf[:])
                        ins.then_inc(sh["tp0b"], 1)
            te.wait_ge(sh["vs0"], 1)
            te.matmul(ps_sc[0:1, 0:1], p0_vsum[:, 0:1], s_onecol[:],
                      start=True, stop=False, skip_group_check=True)
            te.matmul(ps_sc[0:1, 1:2], p0_vsum[:, 1:2], s_onecol[:],
                      start=False, stop=True, skip_group_check=True).then_inc(sh["mm1"], 1)
            te.wait_ge(sh["inv0"], 1)
            te.matmul(ps_sc[:, 0:2], s_onerow[:], p0_inv2[:],
                      start=True, stop=True, skip_group_check=True).then_inc(sh["bc0"], 1)

        def emit_quant_ve(ve, i):
            for hh in range(2):
                j = 2 * i + hh
                sl = slice(hh * (D // 2), (hh + 1) * (D // 2))
                ve.wait_ge(sh["qm"], j + 1)
                vfence(ve, ve.tensor_scalar(p0_qf[:], p0_qf[:],
                                            1.0, -1.0, ALU.min, ALU.max))
                ve.tensor_copy(p0_qi[:], p0_qf[:]).then_inc(sh["qh"], 1)
                ve.wait_ge(sh["qh"], j + 1)
                if hh == 0 and i >= 1:
                    ve.wait_ge(sh["tp0"], 4 * i)
                ins = ve.tensor_copy(p0_qb[:, sl], p0_qi[:])
                if hh == 0:
                    vfence(ve, ins)
                else:
                    ins.then_inc(sh["q0"], 1)
                    ve.wait_ge(sh["q0"], i + 1)

        @block.gpsimd
        def _(g):
            g.wait_ge(sh["st1"], 16)
            g.collective_compute("AllGather", ALU.bypass, replica_groups=GROUPS4,
                                 ins=[cc_xnh[0].opt()], outs=[g_xnh[0].opt()]).then_inc(sh["cc"], 1)
            g.wait_ge(sh["part0"], 1)
            g.dma_start(out=cc_sc_in, in_=p0_scp[:]).then_inc(sh["gd0"], 16)
            g.wait_ge(sh["gd0"], 16)
            g.collective_compute("AllReduce", ALU.add, replica_groups=GROUP8,
                                 ins=[cc_sc_in.opt()], outs=[cc_sc_out.opt()]).then_inc(sh["cc"], 1)
            g.wait_ge(sh["cc"], CC_AR)
            g.dma_start(out=p0_sct[:], in_=cc_sc_out).then_inc(sh["gd0"], 16)
            for i in range(2):
                g.dma_start(out=p0_w[i % 2][:], in_=w_src(i)).then_inc(sh[f"lq{i % 2}"], 16)
            g.wait_ge(sh["st1"], 32)
            g.collective_compute("AllGather", ALU.bypass, replica_groups=GROUPS4,
                                 ins=[cc_xnh[1].opt()], outs=[g_xnh[1].opt()]).then_inc(sh["cc"], 1)
            for i in range(16):
                if i >= 1:
                    g.wait_ge(sh["ev0"], 4 * i)
                    if i >= 2:
                        g.wait_ge(sh["sq"], 16 * (i - 1))
                    g.dma_start(out=w_dst(i - 1).rearrange("(c p) f -> p c f", p=128),
                                in_=p0_wt[:].rearrange("p (c f) -> p c f", c=16)).then_inc(sh["sq"], 16)
                if i + 2 < 16:
                    g.wait_ge(sh["qm"], 2 * (i + 1))
                    g.dma_start(out=p0_w[i % 2][:], in_=w_src(i + 2)).then_inc(sh[f"lq{i % 2}"], 16)
            g.wait_ge(sh["ev0"], 64)
            g.wait_ge(sh["sq"], 16 * 15)
            g.dma_start(out=w_dst(15).rearrange("(c p) f -> p c f", p=128),
                        in_=p0_wt[:].rearrange("p (c f) -> p c f", c=16)).then_inc(sh["sq"], 16)
            g.wait_ge(sh["sq"], 256)
            g.wait_ge(sh["stw0b"], 32)
            g.collective_compute("AllGather", ALU.bypass, replica_groups=GROUP8,
                                 ins=[cc_w1_in.opt()], outs=[g_w1qT.opt()]).then_inc(sh["cc"], 1)
            g.collective_compute("AllGather", ALU.bypass, replica_groups=GROUP8,
                                 ins=[cc_w2_in.opt()], outs=[g_w2qT.opt()]).then_inc(sh["cc"], 1)
            g.wait_ge(sh["zst3"], 32)
            g.collective_compute("AllGather", ALU.bypass, replica_groups=GROUPS4,
                                 ins=[cc_zh[0].opt()], outs=[g_zh[0].opt()]).then_inc(sh["cc"], 1)
            g.collective_compute("AllGather", ALU.bypass, replica_groups=GROUP8,
                                 ins=[cc_wo_in.opt()], outs=[g_woT.opt()]).then_inc(sh["cc"], 1)

        for _g in reversed(p1_tpg):
            _g.__exit__(None, None, None)
        p1.close()

        with contextlib.ExitStack() as p23:
            E23 = p23.enter_context
            s_qT = [E23(nc.sbuf_tensor(f"s_qT{u}", [128, S], BF16)) for u in range(4)]
            s_kT = [E23(nc.sbuf_tensor(f"s_kT{u}", [128, S], BF16)) for u in range(4)]
            s_v = [E23(nc.sbuf_tensor(f"s_v{u}", [128, NKC * 129], BF16)) for u in range(4)]

            with contextlib.ExitStack() as p2:
                E2 = p2.enter_context
                p2_sxn = [E2(nc.sbuf_tensor(f"p2_sxn{i}", [128, 16 * 512], BF16)) for i in range(2)]
                p2_swT = E2(nc.sbuf_tensor("p2_swT", [128, 4 * 2048 + 16 * 512], BF16))
                VOFF = 4 * 2048
                ps_qk = [E2(nc.psum_tensor(f"ps_qk{i}", [128, 512], F32)) for i in range(3)]
                ps_v = [E2(nc.psum_tensor(f"ps_v{i}", [128, 512], F32)) for i in range(2)]

                @block.vector
                def _(ve):
                    ve.wait_ge(sh["xnb1"], 8)
                    ve.wait_ge(sh["st1"], 32)
                    for u in range(4):
                        ve.memset(s_v[u][:], 1.0)

                P2ORDER = [(rk, h) for h in range(2) for rk in range(4)]

                for pas in range(2):
                    u0, u1 = 2 * pas, 2 * pas + 1

                    @block.sync
                    def _(sy, pas=pas, u0=u0, u1=u1):
                        if pas == 0:
                            sy.wait_ge(sh["stw0b"], 224)
                            sy.wait_ge(sh["xnb1"], 8)
                        if pas == 1:
                            sy.wait_ge(sh["mqk2"], 32)
                            sy.wait_ge(sh["mv2"], 32)
                        for k, (w, u) in enumerate([(0, u0), (0, u1), (1, u0), (1, u1)]):
                            src = l_wT[(w * 4 + u) * D:(w * 4 + u + 1) * D, :]
                            sy.dma_start(out=p2_swT[:, k * 2048:(k + 1) * 2048].rearrange("p (c f) -> p c f", c=16),
                                         in_=src.rearrange("(c p) f -> p c f", p=128)).then_inc(sh["wld2"], 16)
                        if pas == 0:
                            for u in range(4):
                                src = l_wT[(2 * 4 + u) * D:(2 * 4 + u + 1) * D, :]
                                dst = p2_swT[:, VOFF:].rearrange("p (c four f) -> p c four f", four=4, f=128)[:, :, u, :]
                                sy.dma_start(out=dst,
                                             in_=src.rearrange("(c p) f -> p c f", p=128)).then_inc(sh["wld2"], 16)
                        sy.wait_ge(sh["cc"], CC_XN0)
                        if pas == 0:
                            sy.wait_ge(sh["st1"], 32)
                        for e, (rk, h) in enumerate(P2ORDER):
                            if e >= 2:
                                if pas == 0:
                                    sy.wait_ge(sh["mv2"], 4 * (e - 1))
                                else:
                                    sy.wait_ge(sh["mqk2"], 32 + 4 * (e - 1))
                            if e == 4:
                                sy.wait_ge(sh["cc"], CC_XN1)
                            src = g_xnh[h][rk * D:(rk + 1) * D, :]
                            sy.dma_start(out=p2_sxn[e % 2][:].rearrange("p (c f) -> p c f", c=16),
                                         in_=src.rearrange("(c p) f -> p c f", p=128)).then_inc(sh[f"ln{e % 2}"], 16)

                    @block.tensor
                    def _(te, pas=pas):
                        if pas == 0:
                            te.wait_ge(sh["ev1"], 32)
                        te.wait_ge(sh["wld2"], 128 + 64 * pas)
                        for tb in range(8):
                            Tg = 8 * pas + tb
                            te.wait_ge(sh[f"ln{Tg % 2}"], 16 * (Tg // 2 + 1))
                            for k in range(4):
                                Gq = 4 * Tg + k
                                if Gq >= 3:
                                    te.wait_ge(sh["eqk2"], Gq - 2)
                                for c in range(16):
                                    ins = te.matmul(ps_qk[Gq % 3][:],
                                                    p2_swT[:, k * 2048 + c * 128: k * 2048 + (c + 1) * 128],
                                                    p2_sxn[tb % 2][:, c * 512:(c + 1) * 512],
                                                    start=(c == 0), stop=(c == 15),
                                                    skip_group_check=True)
                                ins.then_inc(sh["mqk2"], 1)
                            if pas == 0:
                                for ts in range(4):
                                    Gv = 4 * Tg + ts
                                    if Gv >= 2:
                                        te.wait_ge(sh["evv2"], Gv - 1)
                                    for c in range(16):
                                        ins = te.matmul(ps_v[ts % 2][:],
                                                        p2_sxn[tb % 2][:, c * 512 + ts * 128: c * 512 + ts * 128 + 128],
                                                        p2_swT[:, VOFF + c * 512: VOFF + (c + 1) * 512],
                                                        start=(c == 0), stop=(c == 15),
                                                        skip_group_check=True)
                                    ins.then_inc(sh["mv2"], 1)
                            te.wait_ge(sh["q0"], Tg + 1)
                            for g4 in range(4):
                                G = 4 * Tg + g4
                                if G >= 2:
                                    te.wait_ge(sh["ev0"], G - 1)
                                for c4 in range(4):
                                    c = g4 * 4 + c4
                                    ins = te.transpose(ps_tp[g4 % 2][:, c4 * 128:(c4 + 1) * 128],
                                                       p0_qb[:, c * 128:(c + 1) * 128], s_idbf[:])
                                ins.then_inc(sh["tp0"], 1)

                    @block.vector
                    def _(ve, pas=pas, u0=u0, u1=u1):
                        for e, (rk, h) in enumerate(P2ORDER):
                            Tg = 8 * pas + e
                            tb_val = 2 * rk + h
                            emit_quant_ve(ve, Tg)
                            for k, (dst, u) in enumerate([(s_qT, u0), (s_qT, u1), (s_kT, u0), (s_kT, u1)]):
                                Gq = 4 * Tg + k
                                ve.wait_ge(sh["mqk2"], Gq + 1)
                                ve.tensor_copy(dst[u][:, tb_val * 512:(tb_val + 1) * 512],
                                               ps_qk[Gq % 3][:]).then_inc(sh["eqk2"], 1)
                            if pas == 0:
                                for ts in range(4):
                                    Gv = 4 * Tg + ts
                                    ve.wait_ge(sh["mv2"], Gv + 1)
                                    kb = tb_val * 4 + ts
                                    for u in range(3):
                                        ve.tensor_copy(s_v[u][:, kb * 129: kb * 129 + 128],
                                                       ps_v[ts % 2][:, u * 128:(u + 1) * 128])
                                    ve.tensor_copy(s_v[3][:, kb * 129: kb * 129 + 128],
                                                   ps_v[ts % 2][:, 384:512]).then_inc(sh["evv2"], 1)
                            for g4 in range(4):
                                G = 4 * Tg + g4
                                ve.wait_ge(sh["tp0"], G + 1)
                                if Tg >= 1 and g4 == 0:
                                    ve.wait_ge(sh["sq"], 16 * Tg)
                                ve.tensor_copy(p0_wt[:, g4 * 512:(g4 + 1) * 512],
                                               ps_tp[g4 % 2][:, 0:512]).then_inc(sh["ev0"], 1)

            for _g in reversed(_wp_psg):
                _g.__exit__(None, None, None)

            with contextlib.ExitStack() as p3:
                E3 = p3.enter_context
                p3_ssm = [E3(nc.sbuf_tensor(f"p3_ssm{i}", [128, 512], F32)) for i in range(2)]
                p3_E = [E3(nc.sbuf_tensor(f"p3_E{i}", [128, 512], BF16)) for i in range(3)]
                p3_rl = E3(nc.sbuf_tensor("p3_rl", [128, 1], F32))
                p3_ztok = E3(nc.sbuf_tensor("p3_ztok", [128, 32 * 128], BF16))
                p3_zT = E3(nc.sbuf_tensor("p3_zT", [128, S], BF16))
                ps_s = [E3(nc.psum_tensor(f"ps_s{i}", [128, 512], F32)) for i in range(4)]
                ps_u = [E3(nc.psum_tensor(f"ps_u{i}", [128, 2 * 129], F32)) for i in range(2)]
                ps_zt = [E3(nc.psum_tensor(f"ps_zt{i}", [128, 1024], BF16)) for i in range(2)]

                def ps_u_col(q1):
                    return ps_u[q1 // 2][:, (q1 % 2) * 129:(q1 % 2) * 129 + 129]

                def tile_m(tl):
                    return tl["kc"] - 4 * tl["qb"] if tl["diag"] else 0

                @block.tensor
                def _(te):
                    te.wait_ge(sh["eqk2"], 64)
                    te.wait_ge(sh["evv2"], 32)
                    te.wait_ge(sh["ev0"], 64)

                    def emit_av(pv_idx, u):
                        pv = TILES[pv_idx]
                        m = tile_m(pv)
                        te.wait_ge(sh["e3"], pv_idx + 1)
                        if pv["first"]:
                            te.wait_ge(sh["z3"], QBIDX[pv_idx])
                        for q1 in range(m, 4):
                            stop = pv["diag"] and (m == q1)
                            ins = te.matmul(ps_u_col(q1),
                                            p3_E[pv_idx % 3][:, q1 * 128:(q1 + 1) * 128],
                                            s_v[u][:, pv["kc"] * 129: pv["kc"] * 129 + 129],
                                            start=pv["first"], stop=stop,
                                            skip_group_check=True)
                        ins.then_inc(sh["av3"], 1)

                    for u in range(4):
                        prev = None
                        for t in UNIT_T[u]:
                            tl = TILES[t]
                            if t >= 4:
                                t4 = TILES[t - 4]
                                if t4["diag"]:
                                    te.wait_ge(sh["c3"], NDIAG[t - 4])
                                else:
                                    te.wait_ge(sh["e3"], t - 3)
                            te.matmul(ps_s[t % 4][:],
                                      s_kT[u][:, tl["kc"] * 128:(tl["kc"] + 1) * 128],
                                      s_qT[u][:, tl["qb"] * QBW:(tl["qb"] + 1) * QBW],
                                      start=True, stop=True,
                                      skip_group_check=True).then_inc(sh["s3"], 1)
                            if prev is not None:
                                emit_av(prev, u)
                            prev = t
                        emit_av(prev, u)
                        te.wait_ge(sh["z3"], (u + 1) * NQB)
                        for g8 in range(8):
                            Gz = u * 8 + g8
                            if Gz >= 2:
                                te.wait_ge(sh["zev3"], Gz - 1)
                            for c4 in range(4):
                                blk = g8 * 4 + c4
                                ins = te.transpose(ps_zt[g8 % 2][:, c4 * 128:(c4 + 1) * 128],
                                                   p3_ztok[:, blk * 128:(blk + 1) * 128],
                                                   s_idbf[:])
                            ins.then_inc(sh["ztp3"], 1)

                @block.vector
                def _(ve):
                    ve.wait_ge(sh["sq"], 256)
                    for u in range(4):
                        for t in UNIT_T[u]:
                            tl = TILES[t]
                            if tl["diag"]:
                                d = NDIAG[t] - 1
                                m = tile_m(tl)
                                ve.wait_ge(sh["s3"], t + 1)
                                if d >= 2:
                                    ve.wait_ge(sh["e3"], DIAGS[d - 2] + 1)
                                ve.tensor_tensor(p3_ssm[d % 2][:, m * 128:512],
                                                 ps_s[t % 4][:, m * 128:512],
                                                 s_mask[:, m * 512 + m * 128:(m + 1) * 512],
                                                 ALU.add).then_inc(sh["c3"], 1)
                            if tl["last"]:
                                ve.wait_ge(sh["av3"], t + 1)
                                if tl["qb"] == 0 and u >= 1:
                                    ve.wait_ge(sh["ztp3"], u * 8)
                                for q1 in range(4):
                                    vfence(ve, ve.reciprocal(p3_rl[:], ps_u_col(q1)[:, 128:129]))
                                    blk = tl["qb"] * 4 + q1
                                    ins = ve.tensor_scalar(p3_ztok[:, blk * 128:(blk + 1) * 128],
                                                           ps_u_col(q1)[:, 0:128], p3_rl[:],
                                                           None, ALU.mult)
                                ins.then_inc(sh["z3"], 1)
                        for g8 in range(8):
                            Gz = u * 8 + g8
                            ve.wait_ge(sh["ztp3"], Gz + 1)
                            if g8 == 0 and u >= 1:
                                ve.wait_ge(sh["zst3"], 16 * u)
                            for c4 in range(4):
                                blk = g8 * 4 + c4
                                qb, q1 = blk // 4, blk % 4
                                ins = ve.tensor_copy(
                                    p3_zT[:, qb * 512 + q1 * 128: qb * 512 + (q1 + 1) * 128],
                                    ps_zt[g8 % 2][:, c4 * 128:(c4 + 1) * 128])
                            ins.then_inc(sh["zev3"], 1)

                @block.scalar
                def _(sc):
                    sc.wait_ge(sh["ldc"], NCONSTS)
                    for t in range(NT):
                        tl = TILES[t]
                        u = tl["u"]
                        fd = SLOT_FD[u]
                        m = tile_m(tl)
                        if tl["diag"]:
                            sc.wait_ge(sh["c3"], NDIAG[t])
                            src3 = p3_ssm[(NDIAG[t] - 1) % 2]
                            off = 0
                        else:
                            sc.wait_ge(sh["s3"], t + 1)
                            src3 = ps_s[t % 4]
                            off = 0
                        if t >= 3:
                            sc.wait_ge(sh["av3"], t - 2)
                        for qs in range(QBW // fd):
                            lo = max(qs * fd, m * 128)
                            hi = (qs + 1) * fd
                            if lo >= hi:
                                continue
                            ins = sc.activation(p3_E[t % 3][:, lo:hi],
                                                src3[:, lo:hi],
                                                AF.Exp,
                                                bias=s_btab[u][:, tl["col0"] + qs: tl["col0"] + qs + 1],
                                                scale=RS128)
                        ins.then_inc(sh["e3"], 1)

                @block.sync
                def _(sy):
                    for u in range(4):
                        sy.wait_ge(sh["zev3"], 8 * (u + 1))
                        if u >= 1:
                            sy.wait_ge(sh["zst3"], 16 * u)
                        sy.dma_start(out=cc_zh[u // 2][(u % 2) * 128:(u % 2) * 128 + 128, :],
                                     in_=p3_zT[:]).then_inc(sh["zst3"], 16)

        wprep.close()

        with contextlib.ExitStack() as p45:
            E4 = p45.enter_context
            p4_h = E4(nc.sbuf_tensor("p4_h", [128, 64 * 512], BF16))
            p4_xno = E4(nc.sbuf_tensor("p4_xno", [128, 16 * 512], BF16))
            p4_w1 = [E4(nc.sbuf_tensor(f"p4_w1_{i}", [128, 16 * 128], BF16)) for i in range(2)]
            p5_zT = E4(nc.sbuf_tensor("p5_zT", [128, 16 * 512], BF16))
            p5_w2 = [E4(nc.sbuf_tensor(f"p5_w2_{i}", [128, 64 * 128], BF16)) for i in range(2)]
            p5_wo = [E4(nc.sbuf_tensor(f"p5_wo_{i}", [128, 16 * 128], BF16)) for i in range(2)]
            p5_xr = E4(nc.sbuf_tensor("p5_xr", [128, 4 * D], F32))
            p5_sum = [E4(nc.sbuf_tensor(f"p5_sum{i}", [128, 512], F32)) for i in range(2)]
            p5_ob = [E4(nc.sbuf_tensor(f"p5_ob{i}", [128, 512], F32)) for i in range(2)]
            ps_h = [E4(nc.psum_tensor(f"ps_h{i}", [128, 512], F32)) for i in range(2)]
            ps_o = [E4(nc.psum_tensor(f"ps_o{i}", [128, 512], F32)) for i in range(2)]
            ps_t5 = [E4(nc.psum_tensor(f"ps_t5{i}", [128, 512], F32)) for i in range(2)]

            @block.gpsimd
            def _(g):
                g.wait_ge(sh["av3"], NT)
                for tb2 in range(2):
                    g.wait_ge(sh["cc"], CC_XN0 if tb2 == 0 else CC_XN1)
                    if tb2 == 1:
                        g.wait_ge(sh["mm4"], 64)
                    for c in range(16):
                        g.indirect_dma_start(
                            p4_xno[:, c * 512:(c + 1) * 512], None,
                            g_xnh[tb2][0:128, 0:512],
                            IndirectOffsetOnAxis(s_xnidx[:, tb2 * 16 + c: tb2 * 16 + c + 1], 1)
                        ).then_inc(sh["xg4"], 16)
                    if tb2 == 0:
                        g.wait_ge(sh["zst3"], 64)
                        g.collective_compute("AllGather", ALU.bypass, replica_groups=GROUPS4,
                                             ins=[cc_zh[1].opt()], outs=[g_zh[1].opt()]).then_inc(sh["cc"], 1)
                    if tb2 == 1:
                        g.wait_ge(sh["mm5"], 16)
                    for ec in range(16):
                        g.wait_ge(sh["cc"], CC_Z01 if ec < 8 else CC_Z23)
                        g.indirect_dma_start(
                            p5_zT[:, ec * 512:(ec + 1) * 512], None,
                            g_zh[ec // 8][0:128, 0:512],
                            IndirectOffsetOnAxis(s_zidx[:, tb2 * 16 + ec: tb2 * 16 + ec + 1], 1)
                        ).then_inc(sh["zg5"], 16)

            @block.sync
            def _(sy):
                sy.wait_ge(sh["av3"], NT)
                sy.wait_ge(sh["cc"], CC_W1)
                for tb2 in range(2):
                    for fc in range(64):
                        Fg = 64 * tb2 + fc
                        if Fg >= 2:
                            sy.wait_ge(sh["mm4"], Fg - 1)
                        rkf, fl = fc // 8, (fc % 8) * 128
                        src = g_w1qT[rkf * D:(rkf + 1) * D, fl:fl + 128]
                        sy.dma_start(out=p4_w1[fc % 2][:].rearrange("p (c f) -> p c f", c=16),
                                     in_=src.rearrange("(c p) f -> p c f", p=128)).then_inc(sh[f"l4{Fg % 2}"], 16)
                    if tb2 == 0:
                        sy.wait_ge(sh["ztp3"], 32)
                        sy.wait_ge(sh["zst3"], 64)
                        sy.wait_ge(sh["cc"], CC_WO)
                    if tb2 == 1:
                        sy.wait_ge(sh["os0"], 64 * 8)
                        sy.wait_ge(sh["os1"], 64 * 8)
                    for ts in range(4):
                        sy.dma_start(out=p5_xr[:, ts * D:(ts + 1) * D],
                                     in_=x_own[tb2 * 512 + ts * 128: tb2 * 512 + (ts + 1) * 128, :]).then_inc(sh["xl5"], 16)
                    for dc in range(16):
                        Dg = 16 * tb2 + dc
                        if Dg >= 2:
                            sy.wait_ge(sh["mm5"], Dg - 1)
                        rkd, dl = dc // 2, (dc % 2) * 128
                        src2 = g_w2qT[rkd * FF:(rkd + 1) * FF, dl:dl + 128]
                        sy.dma_start(out=p5_w2[dc % 2][:].rearrange("p (c f) -> p c f", c=64),
                                     in_=src2.rearrange("(c p) f -> p c f", p=128)).then_inc(sh[f"l2{Dg % 2}"], 16)
                        srco = g_woT[rkd * D:(rkd + 1) * D, dl:dl + 128]
                        sy.dma_start(out=p5_wo[dc % 2][:].rearrange("p (c f) -> p c f", c=16),
                                     in_=srco.rearrange("(c p) f -> p c f", p=128)).then_inc(sh[f"lo{Dg % 2}"], 16)
                        if dc >= 2:
                            Dg2 = Dg - 2
                            sy.wait_ge(sh["add5"], Dg2 + 1)
                            for ts in range(4):
                                sy.dma_start(out=out_own[tb2 * 512 + ts * 128: tb2 * 512 + (ts + 1) * 128,
                                                         (dc - 2) * 128:(dc - 1) * 128],
                                             in_=p5_ob[Dg2 % 2][:, ts * 128:(ts + 1) * 128]).then_inc(sh[f"os{Dg2 % 2}"], 16)
                    for dc in (14, 15):
                        Dg2 = 16 * tb2 + dc
                        sy.wait_ge(sh["add5"], Dg2 + 1)
                        for ts in range(4):
                            sy.dma_start(out=out_own[tb2 * 512 + ts * 128: tb2 * 512 + (ts + 1) * 128,
                                                     dc * 128:(dc + 1) * 128],
                                         in_=p5_ob[Dg2 % 2][:, ts * 128:(ts + 1) * 128]).then_inc(sh[f"os{Dg2 % 2}"], 16)
                sy.wait_ge(sh["os0"], 64 * 16)
                sy.wait_ge(sh["os1"], 64 * 16)
                if _DEBUG:
                    sy.dma_start(out=d_gxn[0:4 * D, 0:512], in_=g_xnh[0]).then_inc(sh["os0"], 16)
                    sy.dma_start(out=d_gz[0:1024, :], in_=g_zh[0]).then_inc(sh["os0"], 16)
                    sy.dma_start(out=d_w1, in_=cc_w1_in).then_inc(sh["os0"], 16)
                    sy.dma_start(out=d_sc, in_=cc_sc_out).then_inc(sh["os0"], 16)
                    sy.wait_ge(sh["os0"], 64 * 16 + 64)

            @block.tensor
            def _(te):
                for tb2 in range(2):
                    te.wait_ge(sh["xg4"], 256 * (tb2 + 1))
                    for fc in range(64):
                        Fg = 64 * tb2 + fc
                        te.wait_ge(sh[f"l4{Fg % 2}"], 16 * (Fg // 2 + 1))
                        if Fg >= 2:
                            te.wait_ge(sh["gel4"], Fg - 1)
                        for c in range(16):
                            ins = te.matmul(ps_h[fc % 2][:],
                                            p4_w1[fc % 2][:, c * 128:(c + 1) * 128],
                                            p4_xno[:, c * 512:(c + 1) * 512],
                                            start=(c == 0), stop=(c == 15),
                                            skip_group_check=True)
                        ins.then_inc(sh["mm4"], 1)
                    te.wait_ge(sh["zg5"], 256 * (tb2 + 1))
                    te.wait_ge(sh["gel4"], 64 * (tb2 + 1))
                    for dc in range(16):
                        Dg = 16 * tb2 + dc
                        te.wait_ge(sh[f"l2{Dg % 2}"], 16 * (Dg // 2 + 1))
                        te.wait_ge(sh[f"lo{Dg % 2}"], 16 * (Dg // 2 + 1))
                        if Dg >= 2:
                            te.wait_ge(sh["sc5"], Dg - 1)
                        for fc in range(64):
                            te.matmul(ps_o[dc % 2][:],
                                      p5_w2[dc % 2][:, fc * 128:(fc + 1) * 128],
                                      p4_h[:, fc * 512:(fc + 1) * 512],
                                      start=(fc == 0), stop=False,
                                      skip_group_check=True)
                        for ec in range(16):
                            ins = te.matmul(ps_o[dc % 2][:],
                                            p5_wo[dc % 2][:, ec * 128:(ec + 1) * 128],
                                            p5_zT[:, ec * 512:(ec + 1) * 512],
                                            start=False, stop=(ec == 15),
                                            skip_group_check=True)
                        ins.then_inc(sh["mm5"], 1)
                        te.wait_ge(sh["sc5"], Dg + 1)
                        if Dg >= 2:
                            te.wait_ge(sh["add5"], Dg - 1)
                        for ts in range(4):
                            ins = te.transpose(ps_t5[dc % 2][:, ts * 128:(ts + 1) * 128],
                                               p5_sum[dc % 2][:, ts * 128:(ts + 1) * 128],
                                               s_idf32[:])
                        ins.then_inc(sh["tp5"], 1)

            @block.scalar
            def _(sc):
                for tb2 in range(2):
                    for fc in range(64):
                        Fg = 64 * tb2 + fc
                        sc.wait_ge(sh["mm4"], Fg + 1)
                        if tb2 == 1 and fc == 0:
                            sc.wait_ge(sh["mm5"], 16)
                        sc.activation(p4_h[:, fc * 512:(fc + 1) * 512], ps_h[fc % 2][:],
                                      AF.Gelu, bias=s_b1[:, fc:fc + 1],
                                      scale=1.0).then_inc(sh["gel4"], 1)
                    for dc in range(16):
                        Dg = 16 * tb2 + dc
                        sc.wait_ge(sh["mm5"], Dg + 1)
                        if Dg >= 2:
                            sc.wait_ge(sh["tp5"], Dg - 1)
                        sc.activation(p5_sum[dc % 2][:], ps_o[dc % 2][:], AF.Identity,
                                      bias=s_b2[:, dc:dc + 1],
                                      scale=1.0).then_inc(sh["sc5"], 1)

            @block.vector
            def _(ve):
                for tb2 in range(2):
                    for dc in range(16):
                        Dg = 16 * tb2 + dc
                        ve.wait_ge(sh["tp5"], Dg + 1)
                        if dc == 0:
                            ve.wait_ge(sh["xl5"], 64 * (tb2 + 1))
                        if Dg >= 2:
                            ve.wait_ge(sh[f"os{Dg % 2}"], 64 * (Dg // 2))
                        for ts in range(4):
                            ins = ve.tensor_tensor(p5_ob[dc % 2][:, ts * 128:(ts + 1) * 128],
                                                   p5_xr[:, ts * D + dc * 128: ts * D + (dc + 1) * 128],
                                                   ps_t5[dc % 2][:, ts * 128:(ts + 1) * 128],
                                                   ALU.add)
                        ins.then_inc(sh["add5"], 1)

    return nc



def _stage_inputs(L0_attn_resid_pre, L0_attn_Wq, L0_attn_Wk, L0_attn_Wv,
                  L0_attn_Wo, L0_ffn_w1, L0_ffn_b1, L0_ffn_w2, L0_ffn_b2):
    x2d = np.ascontiguousarray(np.asarray(L0_attn_resid_pre, dtype=np.float32).reshape(T, D))
    Wq = np.ascontiguousarray(np.asarray(L0_attn_Wq, dtype=np.float32))
    Wk = np.ascontiguousarray(np.asarray(L0_attn_Wk, dtype=np.float32))
    Wv = np.ascontiguousarray(np.asarray(L0_attn_Wv, dtype=np.float32))
    Wo = np.ascontiguousarray(np.asarray(L0_attn_Wo, dtype=np.float32))
    w1 = np.ascontiguousarray(np.asarray(L0_ffn_w1, dtype=np.float32))
    b1 = np.ascontiguousarray(np.asarray(L0_ffn_b1, dtype=np.float32))
    w2 = np.ascontiguousarray(np.asarray(L0_ffn_w2, dtype=np.float32))
    b2 = np.ascontiguousarray(np.asarray(L0_ffn_b2, dtype=np.float32))
    id_bf = np.eye(128, dtype=ml_dtypes.bfloat16)
    id_f32 = np.eye(128, dtype=np.float32)

    in_maps = []
    for r in range(8):
        heads = [s * 4 + (r % 4) for s in range(4)]

        def hrows(W):
            return np.ascontiguousarray(
                np.concatenate([W[h * 128:(h + 1) * 128, :] for h in heads], axis=0))

        btab = {f"btab{s}": build_bias_table(heads[s]) for s in range(4)}
        masks = np.zeros((128, 4 * 512), dtype=np.float32)
        jp_ = np.arange(128)[:, None]
        qf_ = np.arange(512)[None, :]
        for m in range(4):
            masks[:, m * 512:(m + 1) * 512] = np.where(jp_ + m * 128 > qf_, NEGBIG, 0.0)
        xnidx = np.zeros((128, 32), dtype=np.int32)
        zidx = np.zeros((128, 32), dtype=np.int32)
        p = np.arange(128)
        for tb2 in range(2):
            for c in range(16):
                rows = (r % 4) * D + c * 128 + p
                xnidx[:, tb2 * 16 + c] = rows * 512
            for ec in range(16):
                rows = (ec % 4) * 256 + ((ec // 4) % 2) * 128 + p
                zidx[:, tb2 * 16 + ec] = rows * S + (r % 4) * 1024 + tb2 * 512
        in_maps.append(dict(
            x_own=x2d[r * TC:(r + 1) * TC],
            wq_u=hrows(Wq), wk_u=hrows(Wk), wv_u=hrows(Wv),
            wo_own=np.ascontiguousarray(Wo[r * 256:(r + 1) * 256, :]),
            w1_own=np.ascontiguousarray(w1[r * 1024:(r + 1) * 1024, :]),
            w2_own=np.ascontiguousarray(w2[r * 256:(r + 1) * 256, :]),
            b1_in=np.ascontiguousarray(b1.reshape(FF // 128, 128).T), b2_in=np.ascontiguousarray(b2.reshape(D // 128, 128).T), id_bf=id_bf, id_f32=id_f32,
            zidx=zidx, xnidx=xnidx, masks=masks, **btab,
        ))
    return in_maps


_NC_CACHE = []


def kernel(**inputs):
    in_maps = _stage_inputs(**inputs)
    if not _NC_CACHE:
        _NC_CACHE.append(build_program())
    nc = _NC_CACHE[0]
    res = run_bass_kernel_spmd(nc, in_maps, core_ids=list(range(8)))
    out = np.concatenate([res.results[r]["out_own"] for r in range(8)], axis=0)
    return np.ascontiguousarray(out.reshape(B, S, D)).astype(np.float32)
